# version 17
# baseline (speedup 1.0000x reference)
import numpy as np
import ml_dtypes
import concourse.bass as bass
import concourse.mybir as mybir
from concourse.bass_utils import run_bass_kernel_spmd

F32 = mybir.dt.float32
BF16 = mybir.dt.bfloat16
AF = mybir.ActivationFunctionType
ALU = mybir.AluOpType
AX = mybir.AxisListType

H, G, D, HID = 32, 8, 128, 4096
S = 4096
SCALE = float(1.0 / np.sqrt(np.float64(D)))
EPS_LN = 1e-5
QKEYS = [f"q_{h}_{k}" for h in range(H) for k in range(8)]


class Prog:

    ENGS = ("pe", "act", "dve", "pool", "sp")
    HANDLE = {"pe": "tensor", "act": "scalar", "dve": "vector",
              "pool": "gpsimd", "sp": "sync"}

    def __init__(self, nc):
        self.nc = nc
        self.ops = []
        self.fence_marks = []

    def add(self, engine, fn, reads=(), writes=(), dma=False, out=False, tag=""):
        self.ops.append(dict(e=engine, fn=fn, r=tuple(reads), w=tuple(writes),
                             dma=dma, out=out, tag=tag, idx=len(self.ops)))

    def fence(self):
        self.fence_marks.append(len(self.ops))

    def emit(self):
        nc = self.nc
        out_keys = sorted({k for op in self.ops if op["out"] for k in op["w"]})
        if out_keys:
            self.add("sp", lambda e: e.nop(hint="final"), reads=out_keys, tag="final")
        n = len(self.ops)
        deps = [set() for _ in range(n)]
        last_w, readers = {}, {}
        fence_at = {}
        for m in self.fence_marks:
            last = {}
            for op in self.ops[:m]:
                last[op["e"]] = op["idx"]
            fence_at[m] = last
        marks = sorted(self.fence_marks)
        mi = 0
        active = {}
        for op in self.ops:
            i = op["idx"]
            while mi < len(marks) and marks[mi] <= i:
                active = fence_at[marks[mi]]
                mi += 1
            for e, j in active.items():
                if e != op["e"]:
                    deps[i].add(j)
            for r in op["r"]:
                if r in last_w:
                    deps[i].add(last_w[r])
            for w in op["w"]:
                for jj in readers.get(w, ()):
                    deps[i].add(jj)
                if w in last_w:
                    deps[i].add(last_w[w])
            for w in op["w"]:
                last_w[w] = i
                readers[w] = []
            for r in op["r"]:
                readers.setdefault(r, []).append(i)
        for i, op in enumerate(self.ops):
            if op["e"] == "pe" and not op["dma"]:
                deps[i] = {j for j in deps[i]
                           if not (self.ops[j]["e"] == "pe" and not self.ops[j]["dma"])}
            deps[i].discard(i)
        needed = set()
        for i in range(n):
            needed.update(deps[i])
        for op in self.ops:
            if op["dma"]:
                needed.add(op["idx"])
        NSLOT = 8
        semval, cnt, dma_ctr, prewait = {}, {}, {}, {}
        for op in self.ops:
            i = op["idx"]
            if op["dma"]:
                s = dma_ctr.get(op["e"], 0)
                dma_ctr[op["e"]] = s + 1
                key = (op["e"], s % NSLOT)
                prev = cnt.get(key, 0)
                prewait[i] = (key, prev)
                cnt[key] = prev + 16
                semval[i] = (key, cnt[key])
            elif i in needed:
                key = op["e"]
                cnt[key] = cnt.get(key, 0) + 1
                semval[i] = (key, cnt[key])
        pe_next, pend = {}, []
        for op in self.ops:
            if op["e"] == "pe" and not op["dma"]:
                pend.append(op["idx"])
                if op["idx"] in semval:
                    for j in pend:
                        pe_next[j] = op["idx"]
                    pend = []
        sems, ctxs = {}, []
        sem_keys = [e for e in self.ENGS] + \
                   [(e, s) for e in dma_ctr for s in range(NSLOT)]
        for kk in sem_keys:
            nm = kk if isinstance(kk, str) else f"{kk[0]}_d{kk[1]}"
            cm = nc.semaphore(f"sem_{nm}")
            sems[kk] = cm.__enter__()
            ctxs.append(cm)
        waited = {e: {} for e in self.ENGS}
        bad = []

        block_cm = nc.Block()
        block = block_cm.__enter__()

        def make_body(ename):
            def body(eng):
                for op in self.ops:
                    if op["e"] != ename:
                        continue
                    i = op["idx"]
                    for j in sorted(deps[i]):
                        if j not in semval:
                            j2 = pe_next.get(j)
                            if j2 is None or j2 >= i:
                                if self.ops[j]["e"] == "pe" and ename != "pe" and j2 is None:
                                    bad.append((i, j, op["tag"]))
                                continue
                            j = j2
                        key, v = semval[j]
                        if waited[ename].get(key, 0) < v:
                            eng.wait_ge(sems[key], v)
                            waited[ename][key] = v
                    if i in prewait:
                        key, v = prewait[i]
                        if v > 0 and waited[ename].get(key, 0) < v:
                            eng.wait_ge(sems[key], v)
                            waited[ename][key] = v
                    ins = op["fn"](eng)
                    if i in semval:
                        assert ins is not None, f"op {i} ({op['tag']}) returned None"
                        key, _ = semval[i]
                        ins.then_inc(sems[key], 16 if op["dma"] else 1)
            return body

        for e in self.ENGS:
            if any(op["e"] == e for op in self.ops):
                getattr(block, self.HANDLE[e])(make_body(e))
        block_cm.__exit__(None, None, None)
        for cm in reversed(ctxs):
            cm.__exit__(None, None, None)
        assert not bad, f"unresolvable PE deps: {bad[:4]}"


def alloc_bufs(nc):
    sb = {}
    R1 = nc.alloc_sbuf_tensor("R1", [128, 32768], BF16).ap()
    R2 = nc.alloc_sbuf_tensor("R2", [128, 32768], BF16).ap()
    SCR = nc.alloc_sbuf_tensor("SCR", [128, 28672], BF16).ap()
    CP = nc.alloc_sbuf_tensor("CP", [128, 9600], BF16).ap()
    sb["zwT"] = nc.alloc_sbuf_tensor("zwT", [32, 1024], BF16).ap()
    sb["rcp"] = nc.alloc_sbuf_tensor("rcp", [128, 8], F32).ap()
    sb["cfs"] = nc.alloc_sbuf_tensor("cfs", [128, 256], F32).ap()
    scr32 = SCR.bitcast(F32)
    sb["wbufA"] = SCR[:, 0:8192]
    sb["wbufB"] = SCR[:, 8192:16384]
    sb["v_stage"] = SCR[:, 16384:24640]
    sb["kstage"] = [SCR[:, 24640:25664], SCR[:, 25664:26688]]
    rw = lambda i: scr32[:, 8192 + i * 512: 8192 + (i + 1) * 512]
    sb["ropews"] = {"e1": [rw(0), rw(3)], "sh": [rw(1), rw(4)], "t1": [rw(2), rw(5)]}
    sb["KgA"], sb["KgB"] = SCR[:, 0:4096], SCR[:, 4096:8192]
    sb["VgA"], sb["VgB"] = SCR[:, 8192:12320], SCR[:, 12320:16448]
    sb["eA"], sb["eB"] = SCR[:, 16448:17472], SCR[:, 17472:18496]
    sb["wo_sb"] = [SCR[:, 18496:22592], SCR[:, 22592:26688]]
    sb["d2T"] = [SCR[:, 0:1024], SCR[:, 1024:2048]]
    sb["r1bf"] = [SCR[:, 2048:3072], SCR[:, 3072:4096]]
    sb["xTsl"] = [scr32[:, 2048:3072], scr32[:, 3072:4096]]
    sb["r1ev"] = [scr32[:, 4096:5120], scr32[:, 5120:6144]]
    sb["mrep"] = scr32[:, 6144:7168]
    sb["rrep"] = scr32[:, 7168:8192]
    r132 = R1.bitcast(F32)
    sb["r1ap"] = [r132[:, 8768 + i * 1024: 8768 + (i + 1) * 1024] for i in range(6)] \
        + [scr32[:, 4096:5120], scr32[:, 5120:6144]]
    sb["r1ap_keys"] = ["rA0", "rA1", "rA2", "rA3", "rA4", "rA5", "r1e0", "r1e1"]
    sb["ctxT"] = R1
    sb["bcat_sb"] = R1[0:32, 0:4096]
    sb["ztf"] = r132[0:36, 2048:3072]
    sb["al32"] = r132[0:32, 3072:4096]
    sb["alw"] = r132[0:4, 4096:5120]
    sb["lnA"] = r132[0:1, 5120:6144]
    sb["lnB"] = r132[0:1, 6144:7168]
    sb["asm"] = R1[:, 16384:17536]
    sb["q"] = R2
    sb["x1T"] = R2
    r232 = R2.bitcast(F32)
    sb["wkR2"] = [R2[:, i * 8192:(i + 1) * 8192] for i in range(4)]
    sb["hstg"] = [r232[:, 8192:12288], r232[:, 12288:16384]]
    sb["wqC"] = R2[:, 24576:32768]
    sb["wqD"] = R2[:, 16384:24576]
    sb["ctxnat"] = [[R2[:, h * 1024 + k * 128: h * 1024 + (k + 1) * 128]
                     for k in range(8)] for h in range(H)]
    cp32 = CP.bitcast(F32)
    sb["ropeC"] = cp32[:, 0:1024]
    sb["ropeS"] = cp32[:, 1024:2048]
    sb["bv"] = cp32[:, 2048:3072]
    sb["bqT"] = cp32[:, 3072:3104]
    sb["bkT"] = cp32[:, 3104:3112]
    sb["masks"] = CP[:, 8192:8704]
    sb["identcv"] = CP[:, 8704:8960]
    sb["ident"] = CP[:, 8704:8832]
    sb["onesc"] = CP[:, 8832:8960]
    cfs = sb["cfs"]
    sb["boT"] = cfs[:, 0:32]
    sb["bbT"] = cfs[:, 32:64]
    sb["ones1f"] = cfs[0:1, 64:192]
    sb["ones4f"] = cfs[0:4, 64:65]
    sb["ones14"] = cfs[0:1, 64:68]
    sb["E432"] = cfs[0:4, 192:224]
    sb["rbp4"] = cfs[0:4, 224:225]
    sb["prp4"] = cfs[0:4, 225:226]
    sb["epsb"] = cfs[0:1, 226:227]
    return sb


def alloc_psums(nc, sb):
    sb["scA"] = nc.alloc_psum_tensor("scA", [128, 1024], F32).ap()
    sb["scB"] = nc.alloc_psum_tensor("scB", [128, 1024], F32).ap()
    for k in range(4):
        sb[f"pctx{k}"] = nc.alloc_psum_tensor(f"pctx{k}", [128, 512], F32).ap()
    sb["ps512"] = [sb["scA"][:, 0:512], sb["scA"][:, 512:1024],
                   sb["scB"][:, 0:512], sb["scB"][:, 512:1024],
                   sb["pctx0"][:, 0:512], sb["pctx1"][:, 0:512],
                   sb["pctx2"][:, 0:512], sb["pctx3"][:, 0:512]]
    sb["ps512k"] = ["psA0", "psA1", "psB0", "psB1", "pp4", "pp5", "pp6", "pp7"]
    sb["pctx"] = [(f"pctx{k}", sb[f"pctx{k}"][:, 0:129]) for k in range(4)]
    sb["ptrA"] = sb["pctx2"].bitcast(BF16)[:, 0:128]
    sb["ptrB"] = sb["pctx3"].bitcast(BF16)[:, 0:128]
    sb["pso"] = [("pctx0", sb["pctx0"]), ("pctx1", sb["pctx1"])]
    sb["meanps"] = sb["scA"][0:1, 0:1024]
    sb["varps"] = sb["scB"][0:1, 0:1024]
    sb["psbc"] = sb["scB"]
    sb["psz"] = sb["scA"][0:36, 0:1024]


def emit_rope_evac(p, sb, ps_key, ps_ap, dst_ap, dst_keys, bias_ap, half, par,
                   tag, war_vstg=False):
    e1 = sb["ropews"]["e1"][par]
    sh = sb["ropews"]["sh"][par]
    t1 = sb["ropews"]["t1"][par]
    c_ap = sb["ropeC"][:, half * 512: half * 512 + 512]
    s_ap = sb["ropeS"][:, half * 512: half * 512 + 512]
    ek, shk, tk = f"re1_{par}", f"rsh_{par}", f"rt1_{par}"
    vw = ["vstg"] if war_vstg else []
    p.add("act", lambda e: e.activation(e1, ps_ap, AF.Identity, bias=bias_ap),
          reads=[ps_key, "c_b"], writes=[ek] + vw, tag=f"{tag}e1")
    p.add("sp", lambda e: e.dma_start(out=sh[0:64, :], in_=e1[64:128, :]),
          reads=[ek], writes=[shk] + vw, dma=True, tag=f"{tag}sl")
    p.add("sp", lambda e: e.dma_start(out=sh[64:128, :], in_=e1[0:64, :]),
          reads=[ek], writes=[shk] + vw, dma=True, tag=f"{tag}sh")
    p.add("dve", lambda e: e.tensor_mul(t1, e1, c_ap),
          reads=[ek, "c_rope"], writes=[tk] + vw, tag=f"{tag}t1")
    p.add("dve", lambda e: e.tensor_mul(sh, sh, s_ap),
          reads=[shk, "c_rope"], writes=[shk], tag=f"{tag}t2")
    p.add("dve", lambda e: e.tensor_add(dst_ap, t1, sh),
          reads=[tk, shk], writes=list(dst_keys), tag=f"{tag}su")


def p2_group_load(p, sb, io, g, eng):
    par = g % 2
    Kg = [sb["KgA"], sb["KgB"]]
    Vg = [sb["VgA"], sb["VgB"]]
    kx = ["wqA"] if g < 2 else []
    vx = (["wqB"] + (["re1_0"] if g == 1 else [])) if g < 2 else []
    for j in range(4):
        p.add(eng, lambda e, g=g, j=j, par=par: e.dma_start(
            out=Kg[par].rearrange("p (b m) -> p b m", m=128)[:, j:32:4, :],
            in_=io["kv_gK"][j * 1024 + g * 128: j * 1024 + (g + 1) * 128, :]
                .rearrange("p (k m) -> p k m", m=128)),
            reads=[f"kg{par}", "kvgK"],
            writes=[f"kg{par}"] + (kx if j == 0 else []), dma=True, tag=f"ldK{g}_{j}")
        p.add(eng, lambda e, g=g, j=j, par=par: e.dma_start(
            out=Vg[par].rearrange("p (b c) -> p b c", c=129)[:, j:32:4, :],
            in_=io["kv_gV"][j * 1024:(j + 1) * 1024, g * 129:(g + 1) * 129]
                .rearrange("(k p) c -> p k c", p=128)),
            reads=[f"vg{par}", "kvgV"],
            writes=[f"vg{par}"] + (vx if j == 0 else []), dma=True, tag=f"ldV{g}_{j}")


def build_p1(p, io, sb, mid_cb=None):
    wb = [sb["wbufA"], sb["wbufB"]]
    ps = sb["ps512"]
    psk = sb["ps512k"]
    p.add("pool", lambda e: e.dma_start(
        out=wb[0][:].rearrange("p (c m) -> p c m", c=32),
        in_=io["wv"][:, 0:256].rearrange("(c p) m -> p c m", p=128)),
        reads=["wb0"], writes=["wb0"], dma=True, tag="ldwv0")
    for grp in range(4):
        p.add("pool", lambda e, grp=grp: e.dma_start(
            out=sb["hT"][:, grp * 4096:(grp + 1) * 4096].rearrange(
                "p (c m) -> p c m", c=4),
            in_=io["hT"][grp * 512:(grp + 1) * 512, :].rearrange(
                "(c p) m -> p c m", p=128)),
            writes=[f"hT{grp}"], dma=True, tag=f"ldh{grp}")
    p.add("sp", lambda e: e.dma_start(out=sb["ropeC"], in_=io["ropeC"]),
          writes=["c_rope"], dma=True, tag="ld_rc")
    p.add("sp", lambda e: e.dma_start(out=sb["ropeS"], in_=io["ropeS"]),
          writes=["c_rope"], dma=True, tag="ld_rs")
    p.add("sp", lambda e: e.dma_start(out=sb["bv"], in_=io["bv_rep"]),
          writes=["c_bv"], dma=True, tag="ld_bv")
    p.add("sp", lambda e: e.dma_start(out=sb["bqT"], in_=io["bqT"]),
          writes=["c_b"], dma=True, tag="ld_bq")
    p.add("sp", lambda e: e.dma_start(out=sb["bkT"], in_=io["bkT"]),
          writes=["c_b"], dma=True, tag="ld_bk")
    for grp in range(4, 8):
        par = grp % 2
        stg, skey = sb["hstg"][par], f"hstg{par}"
        p.add("sp", lambda e, grp=grp, stg=stg: e.dma_start(
            out=stg[:].rearrange("p (c m) -> p c m", c=4),
            in_=io["hT"][grp * 512:(grp + 1) * 512, :].rearrange(
                "(c p) m -> p c m", p=128)),
            reads=[skey], writes=[skey], dma=True, tag=f"ldhs{grp}")
        p.add("act", lambda e, grp=grp, stg=stg: e.activation(
            sb["hT"][:, grp * 4096:(grp + 1) * 4096], stg[:], AF.Identity),
            reads=[skey], writes=[f"hT{grp}"], tag=f"hcast{grp}")
    vst = sb["v_stage"]
    p.add("dve", lambda e: e.memset(
        vst.rearrange("p (t b c) -> p t b c", b=8, c=129)[:, :, :, 128:129], 1.0),
        writes=["vstg"], tag="ones_v")
    for q in range(4):
        par = q % 2
        if q > 0:
            p.add("pool", lambda e, q=q, par=par: e.dma_start(
                out=wb[par][:].rearrange("p (c m) -> p c m", c=32),
                in_=io["wv"][:, q * 256:(q + 1) * 256].rearrange("(c p) m -> p c m", p=128)),
                reads=[f"wb{par}"], writes=[f"wb{par}"], dma=True, tag=f"ldwv{q}")
        for tt in range(8):
            pp = (q * 8 + tt) % 8
            for c in range(32):
                p.add("pe", lambda e, par=par, tt=tt, c=c, pp=pp: e.matmul(
                    ps[pp][:, 0:256],
                    sb["hT"][:, c * 1024 + tt * 128: c * 1024 + tt * 128 + 128],
                    wb[par][:, c * 256:(c + 1) * 256],
                    start=(c == 0), stop=(c == 31)),
                    reads=[f"hT{c // 4}", f"wb{par}"], writes=[psk[pp]],
                    tag=f"vmm{q}_{tt}_{c}")
            dst = vst[:, tt * 1032:(tt + 1) * 1032].rearrange(
                "p (b c) -> p b c", c=129)[:, 2 * q:2 * q + 2, 0:128]
            p.add("dve", lambda e, q=q, pp=pp, dst=dst: e.tensor_add(
                dst, ps[pp][:, 0:256].rearrange("p (b c) -> p b c", c=128),
                sb["bv"][:, q * 256:(q + 1) * 256].rearrange("p (b c) -> p b c", c=128)),
                reads=[psk[pp], "c_bv"], writes=["vstg"], tag=f"vev{q}_{tt}")
    for gp in range(4):
        extra = ["hstg0"] if gp == 2 else (["hstg1"] if gp == 3 else [])
        p.add("pool", lambda e, gp=gp: e.dma_start(
            out=sb["wkR2"][gp][:].rearrange("p (c m) -> p c m", c=32),
            in_=io["wk"][:, gp * 256:(gp + 1) * 256].rearrange("(c p) m -> p c m", p=128)),
            writes=[f"wkR2_{gp}"] + extra, dma=True, tag=f"ldwk{gp}")
    p.add("sp", lambda e: e.dma_start(
        out=io["kv_ownV"][:].rearrange("(t p) c -> p t c", p=128),
        in_=vst.rearrange("p (t c) -> p t c", t=8)),
        reads=["vstg"], writes=["kvownV"], dma=True, tag="st_v")
    p.add("pool", lambda e: e.collective_compute(
        "AllGather", ALU.bypass, replica_groups=[[0, 1, 2, 3], [4, 5, 6, 7]],
        ins=[io["kv_ownV"].opt()], outs=[io["kv_gV"].opt()]),
        reads=["kvownV"], writes=["kvgV"], tag="agV")
    for gp in range(4):
        for g in (2 * gp, 2 * gp + 1):
            for half in range(2):
                pp = (g * 2 + half) % 8
                for c in range(32):
                    p.add("pe", lambda e, gp=gp, g=g, half=half, c=c, pp=pp: e.matmul(
                        ps[pp][:],
                        sb["wkR2"][gp][:, c * 256 + (g % 2) * 128: c * 256 + (g % 2) * 128 + 128],
                        sb["hT"][:, c * 1024 + half * 512: c * 1024 + half * 512 + 512],
                        start=(c == 0), stop=(c == 31)),
                        reads=[f"hT{c // 4}", f"wkR2_{gp}"], writes=[psk[pp]],
                        tag=f"kmm{g}_{half}_{c}")
                emit_rope_evac(p, sb, psk[pp], ps[pp][:],
                               sb["kstage"][g % 2][:, half * 512: half * 512 + 512],
                               [f"kst{g % 2}"], sb["bkT"][:, g:g + 1], half, half,
                               f"k{g}{half}", war_vstg=(g == 0))
            p.add("sp", lambda e, g=g: e.dma_start(
                out=io["kv_ownK"][g * 128:(g + 1) * 128, :], in_=sb["kstage"][g % 2][:]),
                reads=[f"kst{g % 2}"], writes=["kvownK"], dma=True, tag=f"stk{g}")
    QROT = [0, 1, 2, 3, 0, 1, 2, 0, 1, 2, 0, 1, 0, 1, 0, 1]
    wqb = [sb["wbufA"], sb["wbufB"], sb["wqC"], sb["wqD"]]
    wqk = ["wqA", "wqB", "wqC", "wqD"]
    wqx = {0: ["wb0"], 1: ["wb1"], 2: ["wkR2_3"], 3: ["wkR2_2"]}
    for hp in range(16):
        if hp == 4 and mid_cb is not None:
            mid_cb()
        r = QROT[hp]
        p.add("pool", lambda e, hp=hp, r=r: e.dma_start(
            out=wqb[r][:].rearrange("p (c m) -> p c m", c=32),
            in_=io["wq"][:, hp * 256:(hp + 1) * 256].rearrange("(c p) m -> p c m", p=128)),
            writes=[wqk[r]] + wqx[r], dma=True, tag=f"ldwq{hp}")
        for h in (2 * hp, 2 * hp + 1):
            for half in range(2):
                pp = (h * 2 + half) % 8
                for c in range(32):
                    p.add("pe", lambda e, r=r, half=half, c=c, pp=pp, h=h: e.matmul(
                        ps[pp][:],
                        wqb[r][:, c * 256 + (h % 2) * 128: c * 256 + (h % 2) * 128 + 128],
                        sb["hT"][:, c * 1024 + half * 512: c * 1024 + half * 512 + 512],
                        start=(c == 0), stop=(c == 31)),
                        reads=[f"hT{c // 4}", wqk[r]], writes=[psk[pp]],
                        tag=f"qmm{h}_{half}_{c}")
                emit_rope_evac(p, sb, psk[pp], ps[pp][:],
                               sb["q"][:, h * 1024 + half * 512: h * 1024 + half * 512 + 512],
                               [f"q_{h}_{half * 4 + kk}" for kk in range(4)]
                               + [f"wkR2_{h // 8}"],
                               sb["bqT"][:, h:h + 1], half, half, f"q{h}{half}")
    p2_group_load(p, sb, io, 0, "pool")


def build_p2(p, io, sb):
    Kg = [sb["KgA"], sb["KgB"]]
    Vg = [sb["VgA"], sb["VgB"]]
    e_sb = [sb["eA"], sb["eB"]]
    sc = [sb["scA"], sb["scB"]]
    sck = ["scA", "scB"]
    pctx = sb["pctx"]
    q = sb["q"]

    for g in range(8):
        par = g % 2
        if 1 <= g + 1 < 8:
            p2_group_load(p, sb, io, g + 1, "sp")
        for h4 in range(4):
            h = 4 * g + h4
            for s in range(2):
                nb = 16 if s == 0 else 32
                hi = 4 * s + 3
                pending = None
                for pr in range(nb // 2):
                    b0 = 2 * pr
                    cur = pr % 2
                    lo = (b0 // 4) if s == 0 else max(4, b0 // 4)
                    ncols = (hi - lo + 1) * 128
                    qkeys = [f"q_{h}_{k}" for k in range(lo, hi + 1)]
                    for half in range(2):
                        b = b0 + half
                        p.add("pe", lambda e, b=b, lo=lo, ncols=ncols, half=half,
                              cur=cur, h=h, par=par: e.matmul(
                            sc[cur][:, half * 512: half * 512 + ncols],
                            Kg[par][:, b * 128:(b + 1) * 128],
                            q[:, h * 1024 + lo * 128: h * 1024 + lo * 128 + ncols],
                            start=True, stop=True),
                            reads=[f"kg{par}"] + qkeys, writes=[sck[cur]],
                            tag=f"sc{h}_{s}_{b}")
                    src = sc[cur].rearrange("p (x c) -> p x c", x=2)[:, :, 0:ncols]
                    dst = e_sb[cur].rearrange("p (x c) -> p x c", x=2)[:, :, 0:ncols]
                    p.add("act", lambda e, src=src, dst=dst: e.activation(
                        dst, src, AF.Exp, scale=SCALE),
                        reads=[sck[cur]], writes=[f"e{cur}"], tag=f"ex{h}_{s}_{b0}")
                    if (s == 0) or (b0 >= 16):
                        for half in range(2):
                            r = (b0 + half) % 4
                            p.add("dve", lambda e, cur=cur, half=half, r=r: e.tensor_mul(
                                e_sb[cur][:, half * 512: half * 512 + 128],
                                e_sb[cur][:, half * 512: half * 512 + 128],
                                sb["masks"][:, r * 128:(r + 1) * 128]),
                                reads=[f"e{cur}", "c_mi"], writes=[f"e{cur}"],
                                tag=f"mk{h}_{s}_{b0 + half}")
                    if pending is not None:
                        pending()
                    def make_ctx(cur=cur, lo=lo, b0=b0, s=s, par=par, h=h, hi=hi):
                        def emit_ctx():
                            for half in range(2):
                                b = b0 + half
                                for k in range(lo, hi + 1):
                                    kk, kap = pctx[k - 4 * s]
                                    p.add("pe", lambda e, cur=cur, half=half, k=k,
                                          lo=lo, b=b, kap=kap, par=par: e.matmul(
                                        kap,
                                        e_sb[cur][:, half * 512 + (k - lo) * 128:
                                                  half * 512 + (k - lo + 1) * 128],
                                        Vg[par][:, b * 129:(b + 1) * 129],
                                        start=(b == 0), stop=(b == 4 * k + 3)),
                                        reads=[f"e{cur}", f"vg{par}"], writes=[kk],
                                        tag=f"cx{h}_{s}_{b}_{k}")
                        return emit_ctx
                    pending = make_ctx()
                pending()
                for k in range(4 * s, 4 * s + 4):
                    kk, kap = pctx[k - 4 * s]
                    p.add("dve", lambda e, kap=kap, k=k: e.reciprocal(
                        sb["rcp"][:, k:k + 1], kap[:, 128:129]),
                        reads=[kk], writes=["rcp"], tag=f"rc{h}_{k}")
                    p.add("act", lambda e, kap=kap, k=k, h=h: e.activation(
                        sb["ctxnat"][h][k], kap[:, 0:128], AF.Identity,
                        scale=sb["rcp"][:, k:k + 1]),
                        reads=[kk, "rcp"], writes=[f"q_{h}_{k}"], tag=f"cn{h}_{k}")
                    p.add("sp", lambda e, h=h, k=k: e.dma_start(
                        out=sb["ctxT"][:, h * 1024 + k * 128: h * 1024 + (k + 1) * 128],
                        in_=sb["ctxnat"][h][k], transpose=True),
                        reads=[f"q_{h}_{k}"], writes=["ctxT"], dma=True,
                        tag=f"tx{h}_{k}")


def build_ctx_transpose(p, io, sb):
    tr = [("pctx2", sb["ptrA"]), ("pctx3", sb["ptrB"])]
    for h in range(H):
        for k in range(8):
            kk, tap = tr[(h * 8 + k) % 2]
            p.add("pe", lambda e, h=h, k=k, tap=tap: e.transpose(
                tap, sb["ctxnat"][h][k], sb["ident"]),
                reads=[f"q_{h}_{k}", "c_mi"], writes=[kk], tag=f"tp{h}_{k}")
            p.add("dve", lambda e, h=h, k=k, tap=tap: e.tensor_copy(
                sb["ctxT"][:, h * 1024 + k * 128: h * 1024 + (k + 1) * 128], tap),
                reads=[kk], writes=["ctxT"], tag=f"tc{h}_{k}")


def build_p3t(p, io, sb):
    wo_sb = sb["wo_sb"]
    xT = sb["xTsl"]
    r1ev = sb["r1ev"]
    r1bf = sb["r1bf"]
    d2T = sb["d2T"]
    pso = sb["pso"]

    def mm_stage(which, rhs, rhs_key_fn, w_src, bias_ap, x_src, x_key, r_dst,
                 r_key, lora, bf16_to_R2=False, norm_x=False):
        for oc in range(32):
            par = oc % 2
            p.add("pool", lambda e, oc=oc, par=par: e.dma_start(
                out=wo_sb[par][:].rearrange("p (c m) -> p c m", c=32),
                in_=w_src[:, oc * 128:(oc + 1) * 128].rearrange("(c p) m -> p c m", p=128)),
                reads=[f"wo{par}"], writes=[f"wo{par}"], dma=True, tag=f"{which}w{oc}")
            extra = ["vg0", "vg1", "e0", "e1"] if (which == "A" and oc < 2) else []
            p.add("sp", lambda e, oc=oc, par=par: e.dma_start(
                out=xT[par][:], in_=x_src[oc * 128:(oc + 1) * 128, :]),
                reads=[f"xT{par}"] + ([f"{x_key}_{oc}"] if x_key else []) + extra,
                writes=[f"xT{par}"], dma=True, tag=f"{which}x{oc}")
            if norm_x:
                p.add("dve", lambda e, par=par: e.tensor_sub(
                    xT[par][:], xT[par][:], sb["mrep"]),
                    reads=[f"xT{par}", "mrep"], writes=[f"xT{par}"], tag=f"{which}nx{oc}")
                p.add("dve", lambda e, par=par: e.tensor_mul(
                    xT[par][:], xT[par][:], sb["rrep"]),
                    reads=[f"xT{par}", "rrep"], writes=[f"xT{par}"], tag=f"{which}ny{oc}")
            for half in range(2):
                pk, pap = pso[half]
                for c in range(32):
                    p.add("pe", lambda e, par=par, half=half, c=c, pap=pap: e.matmul(
                        pap,
                        wo_sb[par][:, c * 128:(c + 1) * 128],
                        rhs[:, c * 1024 + half * 512: c * 1024 + half * 512 + 512],
                        start=(c == 0), stop=(c == 31 and not lora)),
                        reads=[rhs_key_fn(c), f"wo{par}"], writes=[pk],
                        tag=f"{which}m{oc}_{half}_{c}")
                if lora:
                    p.add("pe", lambda e, oc=oc, half=half, pap=pap: e.matmul(
                        pap, sb["bcat_sb"][0:32, oc * 128:(oc + 1) * 128],
                        sb["zwT"][0:32, half * 512: half * 512 + 512],
                        start=False, stop=True),
                        reads=["c_bc", "zwT"], writes=[pk], tag=f"lr{oc}_{half}")
                p.add("dve", lambda e, oc=oc, par=par, half=half, pap=pap:
                    e.scalar_tensor_tensor(
                        r1ev[par][:, half * 512: half * 512 + 512], pap,
                        bias_ap[:, oc:oc + 1],
                        xT[par][:, half * 512: half * 512 + 512],
                        ALU.add, ALU.add),
                    reads=[pso[half][0], f"xT{par}", "c_f"], writes=[f"r1e{par}"],
                    tag=f"{which}e{oc}_{half}")
            p.add("sp", lambda e, oc=oc, par=par: e.dma_start(
                out=r_dst[oc * 128:(oc + 1) * 128, :], in_=r1ev[par][:]),
                reads=[f"r1e{par}"], writes=[f"{r_key}_{oc}"], dma=True,
                tag=f"{which}s{oc}")
            if bf16_to_R2:
                bdst, bkey = sb["x1T"][:, oc * 1024:(oc + 1) * 1024], f"x1T_{oc}"
            else:
                bdst, bkey = r1bf[par][:], f"r1b{par}"
            p.add("act", lambda e, bdst=bdst, par=par: e.activation(
                bdst, r1ev[par][:], AF.Identity),
                reads=[f"r1e{par}"], writes=[bkey], tag=f"{which}b{oc}")
            p.add("act", lambda e, par=par: e.activation(
                d2T[par][:], r1ev[par][:], AF.Square),
                reads=[f"r1e{par}"], writes=[f"d2{par}"], tag=f"{which}d{oc}")
            for half in range(2):
                bsrc = bdst if bf16_to_R2 else r1bf[par][:]
                p.add("pe", lambda e, oc=oc, half=half, bsrc=bsrc: e.matmul(
                    sb["meanps"][:, half * 512: half * 512 + 512],
                    sb["onesc"][:, 0:1],
                    bsrc[:, half * 512: half * 512 + 512],
                    start=(oc == 0), stop=(oc == 31)),
                    reads=[bkey, "c_mi"], writes=["scA"],
                    tag=f"{which}sm{oc}_{half}")
                p.add("pe", lambda e, oc=oc, par=par, half=half: e.matmul(
                    sb["varps"][:, half * 512: half * 512 + 512],
                    sb["onesc"][:, 0:1],
                    d2T[par][:, half * 512: half * 512 + 512],
                    start=(oc == 0), stop=(oc == 31)),
                    reads=[f"d2{par}", "c_mi"], writes=["scB"],
                    tag=f"{which}sv{oc}_{half}")

    def ln_stats_apply(which, r_src, r_key, dst_bf16, dst_bf16_key, dst_f32_dram,
                       dst_f32_key, out_flag):
        lnA, lnB, lnC = sb["lnA"], sb["lnB"], sb["alw"][0:1, :]
        p.add("dve", lambda e: e.tensor_scalar(
            lnA, sb["meanps"], 1.0 / 4096.0, None, ALU.mult),
            reads=["scA"], writes=["lnm"], tag=f"{which}m")
        p.add("dve", lambda e: e.tensor_mul(lnB, lnA, lnA),
            reads=["lnm"], writes=["lnm2"], tag=f"{which}m2")
        p.add("dve", lambda e: e.scalar_tensor_tensor(
            lnB, sb["varps"], 1.0 / 4096.0, lnB, ALU.mult, ALU.subtract),
            reads=["scB", "lnm2"], writes=["lnv"], tag=f"{which}v")
        p.add("act", lambda e: e.activation(
            lnC, lnB, AF.Ln, bias=sb["epsb"]),
            reads=["lnv", "c_f", "alg"], writes=["lnl"], tag=f"{which}l")
        p.add("act", lambda e: e.activation(
            lnB, lnC, AF.Exp, scale=-0.5),
            reads=["lnl"], writes=["lnr"], tag=f"{which}r")
        for half in range(2):
            p.add("pe", lambda e, half=half: e.matmul(
                sb["psbc"][:, half * 512: half * 512 + 512],
                sb["ones1f"], lnA[:, half * 512: half * 512 + 512],
                start=True, stop=True),
                reads=["lnm", "c_f"], writes=["scB"], tag=f"{which}bm{half}")
        p.add("dve", lambda e: e.tensor_copy(sb["mrep"], sb["psbc"][:, 0:1024]),
              reads=["scB"], writes=["mrep"], tag=f"{which}cm")
        for half in range(2):
            p.add("pe", lambda e, half=half: e.matmul(
                sb["psbc"][:, half * 512: half * 512 + 512],
                sb["ones1f"], lnB[:, half * 512: half * 512 + 512],
                start=True, stop=True),
                reads=["lnr", "c_f", "mrep"], writes=["scB"], tag=f"{which}br{half}")
        p.add("dve", lambda e: e.tensor_copy(sb["rrep"], sb["psbc"][:, 0:1024]),
              reads=["scB"], writes=["rrep"], tag=f"{which}cr")
        if dst_f32_dram is None:
            for oc in range(32):
                ch = sb["x1T"][:, oc * 1024:(oc + 1) * 1024]
                p.add("dve", lambda e, ch=ch: e.tensor_sub(ch, ch, sb["mrep"]),
                    reads=[f"x1T_{oc}", "mrep"], writes=[f"x1T_{oc}"],
                    tag=f"{which}as{oc}")
                p.add("dve", lambda e, ch=ch: e.tensor_mul(ch, ch, sb["rrep"]),
                    reads=[f"x1T_{oc}", "rrep"], writes=[f"x1T_{oc}"],
                    tag=f"{which}am{oc}")
            return
        r1a = sb["r1ap"]
        rk = sb["r1ap_keys"]
        for oc in range(32):
            par = oc % 8
            buf = r1a[par]
            p.add("sp", lambda e, oc=oc, buf=buf: e.dma_start(
                out=buf[:], in_=r_src[oc * 128:(oc + 1) * 128, :]),
                reads=[rk[par], f"{r_key}_{oc}"], writes=[rk[par]], dma=True,
                tag=f"{which}al{oc}")
            p.add("dve", lambda e, buf=buf: e.tensor_sub(buf[:], buf[:], sb["mrep"]),
                reads=[rk[par], "mrep"], writes=[rk[par]], tag=f"{which}as{oc}")
            p.add("dve", lambda e, buf=buf: e.tensor_mul(buf[:], buf[:], sb["rrep"]),
                reads=[rk[par], "rrep"], writes=[rk[par]], tag=f"{which}am{oc}")
            p.add("pool" if oc % 2 == 0 else "sp",
                lambda e, oc=oc, buf=buf: e.dma_start(
                    out=dst_f32_dram[oc * 128:(oc + 1) * 128, :], in_=buf[:]),
                reads=[rk[par]], writes=[f"{dst_f32_key}_{oc}"], dma=True,
                out=out_flag, tag=f"{which}af{oc}")

    mm_stage("A", sb["ctxT"], lambda c: "ctxT", io["wo"], sb["boT"], io["hT"],
             None, io["r1_dram"], "r1d", False, bf16_to_R2=True)
    p.fence()
    p.add("pool", lambda e: e.dma_start(out=sb["bcat_sb"], in_=io["bcat"]),
          writes=["c_bc"], dma=True, tag="ld_bc")
    p.add("pool", lambda e: e.dma_start(
        out=sb["asm"][:].rearrange("p (c m) -> p c m", c=32),
        in_=io["asmall"][:].rearrange("(c p) m -> p c m", p=128)),
        writes=["c_as"], dma=True, tag="ld_as")
    ln_stats_apply("L1", io["r1_dram"], "r1d", sb["x1T"],
                   lambda oc: f"x1T_{oc}", None, None, False)
    for half in range(2):
        for c in range(32):
            p.add("pe", lambda e, c=c, half=half: e.matmul(
                sb["psz"][:, half * 512: half * 512 + 512],
                sb["asm"][:, c * 36:(c + 1) * 36],
                sb["x1T"][:, c * 1024 + half * 512: c * 1024 + half * 512 + 512],
                start=(c == 0), stop=(c == 31)),
                reads=["c_as", f"x1T_{c}"], writes=["scA"], tag=f"z{half}_{c}")
    p.add("dve", lambda e: e.tensor_copy(sb["ztf"][0:36, 0:1024],
                                         sb["psz"][0:36, 0:1024]),
          reads=["scA"], writes=["ztf"], tag="zev")
    p.add("sp", lambda e: e.dma_start(out=sb["alw"][0:4, 0:1024],
                                      in_=sb["ztf"][32:36, 0:1024]),
          reads=["ztf"], writes=["alg"], dma=True, tag="alshift")
    al = sb["alw"]
    p.add("dve", lambda e: e.tensor_scalar(
        al[0:4, :], al[0:4, :], sb["rbp4"], None, ALU.add),
        reads=["alg", "c_f"], writes=["alg"], tag="al_rb")
    p.add("act", lambda e: e.activation(al[0:4, :], al[0:4, :], AF.Exp),
          reads=["alg"], writes=["alg"], tag="al_exp")
    p.add("dve", lambda e: e.tensor_scalar(
        al[0:4, :], al[0:4, :], sb["prp4"], None, ALU.mult),
        reads=["alg", "c_rp"], writes=["alg"], tag="al_pr")
    for half in range(2):
        p.add("pe", lambda e, half=half: e.matmul(
            sb["scA"][0:1, half * 512: half * 512 + 512],
            sb["ones4f"], al[0:4, half * 512: half * 512 + 512],
            start=True, stop=True),
            reads=["alg", "c_f"], writes=["scA"], tag=f"alden{half}")
    p.add("act", lambda e: e.activation(sb["lnB"], sb["scA"][0:1, 0:1024], AF.Ln),
          reads=["scA", "lnr"], writes=["alrcp"], tag="alrc1")
    p.add("act", lambda e: e.activation(sb["lnB"], sb["lnB"], AF.Exp, scale=-1.0),
          reads=["alrcp"], writes=["alrcp"], tag="alrc2")
    for half in range(2):
        p.add("pe", lambda e, half=half: e.matmul(
            sb["scB"][0:4, half * 512: half * 512 + 512],
            sb["ones14"], sb["lnB"][:, half * 512: half * 512 + 512],
            start=True, stop=True),
            reads=["alrcp", "c_f"], writes=["scB"], tag=f"albc{half}")
    p.add("dve", lambda e: e.tensor_mul(
        al[0:4, :], al[0:4, :], sb["scB"][0:4, 0:1024]),
        reads=["alg", "scB"], writes=["alg"], tag="al_n")
    for half in range(2):
        p.add("pe", lambda e, half=half: e.matmul(
            sb["scA"][0:32, half * 512: half * 512 + 512],
            sb["E432"], al[0:4, half * 512: half * 512 + 512],
            start=True, stop=True),
            reads=["alg", "c_f"], writes=["scA"], tag=f"alx{half}")
    p.add("dve", lambda e: e.tensor_copy(sb["al32"][0:32, :],
                                         sb["scA"][0:32, 0:1024]),
          reads=["scA"], writes=["al32"], tag="alxc")
    p.add("dve", lambda e: e.scalar_tensor_tensor(
        sb["zwT"][0:32, :], sb["ztf"][0:32, :], 2.0, sb["al32"][0:32, :],
        ALU.mult, ALU.mult),
        reads=["ztf", "al32"], writes=["zwT"], tag="zw")
    mm_stage("B", sb["x1T"], lambda c: f"x1T_{c}", io["wbT"], sb["bbT"],
             io["r1_dram"], "r1d", io["r2_dram"], "r2d", True, norm_x=True)
    ln_stats_apply("L2", io["r2_dram"], "r2d", None, None, io["outT"], "outd", True)


def build_fused():
    nc = bass.Bass("TRN2", target_bir_lowering=False)
    io = {}
    ein = lambda n, s_, d: nc.dram_tensor(n, s_, d, kind="ExternalInput").ap()
    io["hT"] = ein("hT", [4096, 1024], F32)
    io["wq"] = ein("wq", [4096, 4096], F32)
    io["wk"] = ein("wk", [4096, 1024], F32)
    io["wv"] = ein("wv", [4096, 1024], F32)
    io["ropeC"] = ein("ropeC", [128, 1024], F32)
    io["ropeS"] = ein("ropeS", [128, 1024], F32)
    io["bqT"] = ein("bqT", [128, 32], F32)
    io["bkT"] = ein("bkT", [128, 8], F32)
    io["bv_rep"] = ein("bv_rep", [128, 1024], F32)
    io["wo"] = ein("wo", [4096, 4096], F32)
    io["wbT"] = ein("wbT", [4096, 4096], F32)
    io["masks"] = ein("masks", [128, 512], BF16)
    io["identc"] = ein("identc", [128, 256], BF16)
    io["cf32"] = ein("cf32", [128, 256], F32)
    io["asmall"] = ein("asmall", [4096, 36], F32)
    io["bcat"] = ein("bcat", [32, 4096], F32)
    io["outT"] = nc.dram_tensor("outT", [4096, 1024], F32, kind="ExternalOutput").ap()
    io["kv_ownK"] = nc.dram_tensor("kv_ownK", [1024, 1024], BF16, kind="Internal").ap()
    io["kv_ownV"] = nc.dram_tensor("kv_ownV", [1024, 1032], BF16, kind="Internal").ap()
    io["kv_gK"] = nc.dram_tensor("kv_gK", [4096, 1024], BF16, kind="Internal").ap()
    io["kv_gV"] = nc.dram_tensor("kv_gV", [4096, 1032], BF16, kind="Internal").ap()
    io["r1_dram"] = nc.dram_tensor("r1_dram", [4096, 1024], F32, kind="Internal").ap()
    io["r2_dram"] = nc.dram_tensor("r2_dram", [4096, 1024], F32, kind="Internal").ap()
    sb = alloc_bufs(nc)
    alloc_psums(nc, sb)
    sb["hT"] = sb["ctxT"]
    p = Prog(nc)
    p.add("sp", lambda e: e.dma_start(out=sb["masks"], in_=io["masks"]),
          writes=["c_mi"], dma=True, tag="ld_mk")
    p.add("sp", lambda e: e.dma_start(out=sb["identcv"], in_=io["identc"]),
          writes=["c_mi"], dma=True, tag="ld_id")
    p.add("sp", lambda e: e.dma_start(out=sb["cfs"][:], in_=io["cf32"]),
          writes=["c_f"], dma=True, tag="ld_cf")
    p.add("dve", lambda e: e.tensor_scalar(
        sb["prp4"], sb["prp4"], 1e-12, None, ALU.add),
        reads=["c_f"], writes=["c_rp"], tag="preps")

    def collective():
        p.add("pool", lambda e: e.collective_compute(
            "AllGather", ALU.bypass, replica_groups=[[0, 1, 2, 3], [4, 5, 6, 7]],
            ins=[io["kv_ownK"].opt()], outs=[io["kv_gK"].opt()]),
            reads=["kvownK", "kvgV"], writes=["kvgK"], tag="agK")

    build_p1(p, io, sb, mid_cb=collective)
    p.fence()
    build_p2(p, io, sb)
    build_p3t(p, io, sb)
    p.emit()
    return nc


def stage_core_inputs(inputs, c):
    j, bb = c % 4, c // 4
    x = np.asarray(inputs["L0_attn_hidden_states"][bb])
    rows = np.concatenate([np.arange((4 * k + j) * 128, (4 * k + j + 1) * 128)
                           for k in range(8)])
    hT = np.ascontiguousarray(x[rows].T)
    half = D // 2
    inv = 1.0 / (10000.0 ** (np.arange(half, dtype=np.float64) / half))
    ang = rows[:, None].astype(np.float64) * inv[None, :]
    cosT = np.cos(ang).T.astype(np.float32)
    sinT = np.sin(ang).T.astype(np.float32)
    ropeC = np.ascontiguousarray(np.concatenate([cosT, cosT], 0))
    ropeS = np.ascontiguousarray(np.concatenate([-sinT, sinT], 0))
    tri = (np.arange(128)[:, None] <= np.arange(128)[None, :]).astype(np.float32)
    m = np.zeros((128, 512), np.float32)
    for r in range(4):
        if r < j:
            m[:, r * 128:(r + 1) * 128] = 1.0
        elif r == j:
            m[:, r * 128:(r + 1) * 128] = tri
    return dict(
        hT=hT, rows=rows, bb=bb, ropeC=ropeC, ropeS=ropeS,
        masks=np.ascontiguousarray(m.astype(ml_dtypes.bfloat16)),
        bqT=np.ascontiguousarray(np.asarray(inputs["L0_attn_bq"]).reshape(32, 128).T),
        bkT=np.ascontiguousarray(np.asarray(inputs["L0_attn_bk"]).reshape(8, 128).T),
    )


def kernel(**inputs):
    inputs = {k: np.asarray(v) for k, v in inputs.items()}
    cores = list(range(8))
    act = inputs["L0_ffn_active"].astype(np.int64)
    rep = lambda v: np.ascontiguousarray(np.broadcast_to(
        np.asarray(v, np.float32)[None, :], (128, np.asarray(v).shape[0])))
    identc = np.zeros((128, 256), ml_dtypes.bfloat16)
    identc[:, 0:128] = np.eye(128, dtype=ml_dtypes.bfloat16)
    identc[:, 128:256] = np.ones((128, 128), ml_dtypes.bfloat16)
    cf32 = np.zeros((128, 256), np.float32)
    cf32[:, 0:32] = inputs["L0_attn_bo"].reshape(32, 128).T
    cf32[:, 32:64] = inputs["L0_ffn_b_base"].reshape(32, 128).T
    cf32[:, 64:192] = 1.0
    for ee in range(4):
        cf32[ee, 192 + ee * 8:192 + (ee + 1) * 8] = 1.0
    cf32[0:4, 224] = inputs["L0_ffn_router_b"][act].astype(np.float32)
    cf32[0:4, 225] = inputs["L0_ffn_domain_prior"][act].astype(np.float32)
    cf32[:, 226] = EPS_LN
    shared = dict(
        wq=inputs["L0_attn_Wq"], wk=inputs["L0_attn_Wk"], wv=inputs["L0_attn_Wv"],
        wo=inputs["L0_attn_Wo"],
        wbT=np.ascontiguousarray(inputs["L0_ffn_W_base"].T),
        bv_rep=rep(inputs["L0_attn_bv"]),
        cf32=cf32, identc=identc,
        asmall=np.ascontiguousarray(np.concatenate(
            [inputs["L0_ffn_lora_A"][act].reshape(32, HID).T,
             inputs["L0_ffn_router_W"][act].T], axis=1)),
        bcat=np.ascontiguousarray(
            inputs["L0_ffn_lora_B"][act].transpose(0, 2, 1).reshape(32, HID)),
    )
    staged = [stage_core_inputs(inputs, c) for c in cores]
    nc = build_fused()
    in_maps = [{"hT": st["hT"], "wq": shared["wq"], "wk": shared["wk"],
                "wv": shared["wv"], "ropeC": st["ropeC"], "ropeS": st["ropeS"],
                "bqT": st["bqT"], "bkT": st["bkT"], "bv_rep": shared["bv_rep"],
                "wo": shared["wo"], "wbT": shared["wbT"], "masks": st["masks"],
                "identc": shared["identc"], "cf32": shared["cf32"],
                "asmall": shared["asmall"], "bcat": shared["bcat"]}
               for st in staged]
    res = run_bass_kernel_spmd(nc, in_maps, core_ids=cores)
    out = np.zeros((2, S, HID), np.float32)
    for c in cores:
        out[staged[c]["bb"]][staged[c]["rows"]] = np.asarray(res.results[c]["outT"]).T
    return out


# version 18
# speedup vs baseline: 1.0775x; 1.0775x over previous
import numpy as np
import ml_dtypes
import concourse.bass as bass
import concourse.mybir as mybir
from concourse.bass_utils import run_bass_kernel_spmd

F32 = mybir.dt.float32
BF16 = mybir.dt.bfloat16
AF = mybir.ActivationFunctionType
ALU = mybir.AluOpType
AX = mybir.AxisListType

H, G, D, HID = 32, 8, 128, 4096
S = 4096
SCALE = float(1.0 / np.sqrt(np.float64(D)))
EPS_LN = 1e-5
QKEYS = [f"q_{h}_{k}" for h in range(H) for k in range(8)]


class Prog:

    ENGS = ("pe", "act", "dve", "pool", "sp")
    HANDLE = {"pe": "tensor", "act": "scalar", "dve": "vector",
              "pool": "gpsimd", "sp": "sync"}

    def __init__(self, nc):
        self.nc = nc
        self.ops = []
        self.fence_marks = []

    def add(self, engine, fn, reads=(), writes=(), dma=False, out=False, tag=""):
        self.ops.append(dict(e=engine, fn=fn, r=tuple(reads), w=tuple(writes),
                             dma=dma, out=out, tag=tag, idx=len(self.ops)))

    def fence(self):
        self.fence_marks.append(len(self.ops))

    def emit(self):
        nc = self.nc
        out_keys = sorted({k for op in self.ops if op["out"] for k in op["w"]})
        if out_keys:
            self.add("sp", lambda e: e.nop(hint="final"), reads=out_keys, tag="final")
        n = len(self.ops)
        deps = [set() for _ in range(n)]
        last_w, readers = {}, {}
        fence_at = {}
        for m in self.fence_marks:
            last = {}
            for op in self.ops[:m]:
                last[op["e"]] = op["idx"]
            fence_at[m] = last
        marks = sorted(self.fence_marks)
        mi = 0
        active = {}
        for op in self.ops:
            i = op["idx"]
            while mi < len(marks) and marks[mi] <= i:
                active = fence_at[marks[mi]]
                mi += 1
            for e, j in active.items():
                if e != op["e"]:
                    deps[i].add(j)
            for r in op["r"]:
                if r in last_w:
                    deps[i].add(last_w[r])
            for w in op["w"]:
                for jj in readers.get(w, ()):
                    deps[i].add(jj)
                if w in last_w:
                    deps[i].add(last_w[w])
            for w in op["w"]:
                last_w[w] = i
                readers[w] = []
            for r in op["r"]:
                readers.setdefault(r, []).append(i)
        for i, op in enumerate(self.ops):
            if op["e"] == "pe" and not op["dma"]:
                deps[i] = {j for j in deps[i]
                           if not (self.ops[j]["e"] == "pe" and not self.ops[j]["dma"])}
            deps[i].discard(i)
        needed = set()
        for i in range(n):
            needed.update(deps[i])
        for op in self.ops:
            if op["dma"]:
                needed.add(op["idx"])
        NSLOT = 8
        semval, cnt, dma_ctr, prewait = {}, {}, {}, {}
        for op in self.ops:
            i = op["idx"]
            if op["dma"]:
                s = dma_ctr.get(op["e"], 0)
                dma_ctr[op["e"]] = s + 1
                key = (op["e"], s % NSLOT)
                prev = cnt.get(key, 0)
                prewait[i] = (key, prev)
                cnt[key] = prev + 16
                semval[i] = (key, cnt[key])
            elif i in needed:
                key = op["e"]
                cnt[key] = cnt.get(key, 0) + 1
                semval[i] = (key, cnt[key])
        pe_next, pend = {}, []
        for op in self.ops:
            if op["e"] == "pe" and not op["dma"]:
                pend.append(op["idx"])
                if op["idx"] in semval:
                    for j in pend:
                        pe_next[j] = op["idx"]
                    pend = []
        sems, ctxs = {}, []
        sem_keys = [e for e in self.ENGS] + \
                   [(e, s) for e in dma_ctr for s in range(NSLOT)]
        for kk in sem_keys:
            nm = kk if isinstance(kk, str) else f"{kk[0]}_d{kk[1]}"
            cm = nc.semaphore(f"sem_{nm}")
            sems[kk] = cm.__enter__()
            ctxs.append(cm)
        waited = {e: {} for e in self.ENGS}
        bad = []

        block_cm = nc.Block()
        block = block_cm.__enter__()

        def make_body(ename):
            def body(eng):
                for op in self.ops:
                    if op["e"] != ename:
                        continue
                    i = op["idx"]
                    for j in sorted(deps[i]):
                        if j not in semval:
                            j2 = pe_next.get(j)
                            if j2 is None or j2 >= i:
                                if self.ops[j]["e"] == "pe" and ename != "pe" and j2 is None:
                                    bad.append((i, j, op["tag"]))
                                continue
                            j = j2
                        key, v = semval[j]
                        if waited[ename].get(key, 0) < v:
                            eng.wait_ge(sems[key], v)
                            waited[ename][key] = v
                    if i in prewait:
                        key, v = prewait[i]
                        if v > 0 and waited[ename].get(key, 0) < v:
                            eng.wait_ge(sems[key], v)
                            waited[ename][key] = v
                    ins = op["fn"](eng)
                    if i in semval:
                        assert ins is not None, f"op {i} ({op['tag']}) returned None"
                        key, _ = semval[i]
                        ins.then_inc(sems[key], 16 if op["dma"] else 1)
            return body

        for e in self.ENGS:
            if any(op["e"] == e for op in self.ops):
                getattr(block, self.HANDLE[e])(make_body(e))
        block_cm.__exit__(None, None, None)
        for cm in reversed(ctxs):
            cm.__exit__(None, None, None)
        assert not bad, f"unresolvable PE deps: {bad[:4]}"


def alloc_bufs(nc):
    sb = {}
    R1 = nc.alloc_sbuf_tensor("R1", [128, 32768], BF16).ap()
    R2 = nc.alloc_sbuf_tensor("R2", [128, 32768], BF16).ap()
    SCR = nc.alloc_sbuf_tensor("SCR", [128, 28672], BF16).ap()
    CP = nc.alloc_sbuf_tensor("CP", [128, 9600], BF16).ap()
    sb["zwT"] = nc.alloc_sbuf_tensor("zwT", [32, 1024], BF16).ap()
    sb["rcp"] = nc.alloc_sbuf_tensor("rcp", [128, 8], F32).ap()
    sb["cfs"] = nc.alloc_sbuf_tensor("cfs", [128, 256], F32).ap()
    scr32 = SCR.bitcast(F32)
    sb["wbufA"] = SCR[:, 0:8192]
    sb["wbufB"] = SCR[:, 8192:16384]
    sb["v_stage"] = SCR[:, 16384:24640]
    sb["kstage"] = [SCR[:, 24640:25664], SCR[:, 25664:26688]]
    rw = lambda i: scr32[:, 8192 + i * 512: 8192 + (i + 1) * 512]
    sb["ropews"] = {"e1": [rw(0), rw(1), rw(2), rw(3)],
                    "sh": [rw(4), rw(5), rw(6), rw(7)]}
    sb["KgA"], sb["KgB"] = SCR[:, 0:4096], SCR[:, 4096:8192]
    sb["VgA"], sb["VgB"] = SCR[:, 8192:12320], SCR[:, 12320:16448]
    sb["eA"], sb["eB"] = SCR[:, 16448:17472], SCR[:, 17472:18496]
    sb["eC"] = SCR[:, 26688:27712]
    sb["eD"] = CP[:, 6224:7248]
    sb["wo_sb"] = [SCR[:, 18496:22592], SCR[:, 22592:26688]]
    sb["d2T"] = [SCR[:, 0:1024], SCR[:, 1024:2048]]
    sb["r1bf"] = [SCR[:, 2048:3072], SCR[:, 3072:4096]]
    sb["xTsl"] = [scr32[:, 2048:3072], scr32[:, 3072:4096]]
    sb["r1ev"] = [scr32[:, 4096:5120], scr32[:, 5120:6144]]
    sb["mrep"] = scr32[:, 6144:7168]
    sb["rrep"] = scr32[:, 7168:8192]
    r132 = R1.bitcast(F32)
    sb["r1ap"] = [r132[:, 8768 + i * 1024: 8768 + (i + 1) * 1024] for i in range(6)] \
        + [scr32[:, 4096:5120], scr32[:, 5120:6144]]
    sb["r1ap_keys"] = ["rA0", "rA1", "rA2", "rA3", "rA4", "rA5", "r1e0", "r1e1"]
    sb["ctxT"] = R1
    sb["bcat_sb"] = R1[0:32, 0:4096]
    sb["ztf"] = r132[0:36, 2048:3072]
    sb["al32"] = r132[0:32, 3072:4096]
    sb["alw"] = r132[0:4, 4096:5120]
    sb["lnA"] = r132[0:1, 5120:6144]
    sb["lnB"] = r132[0:1, 6144:7168]
    sb["asm"] = R1[:, 16384:17536]
    sb["q"] = R2
    sb["x1T"] = R2
    r232 = R2.bitcast(F32)
    sb["wkR2"] = [R2[:, i * 8192:(i + 1) * 8192] for i in range(4)]
    sb["hstg"] = [r232[:, 8192:12288], r232[:, 12288:16384]]
    sb["wqC"] = R2[:, 24576:32768]
    sb["wqD"] = R2[:, 16384:24576]
    sb["ctxnat"] = [[R2[:, h * 1024 + k * 128: h * 1024 + (k + 1) * 128]
                     for k in range(8)] for h in range(H)]
    cp32 = CP.bitcast(F32)
    sb["ropeC"] = cp32[:, 0:1024]
    sb["ropeS"] = cp32[:, 1024:2048]
    sb["bv"] = cp32[:, 2048:3072]
    sb["bqT"] = cp32[:, 3072:3104]
    sb["bkT"] = cp32[:, 3104:3112]
    sb["masks"] = CP[:, 8192:8704]
    sb["identcv"] = CP[:, 8704:8960]
    sb["ident"] = CP[:, 8704:8832]
    sb["onesc"] = CP[:, 8832:8960]
    cfs = sb["cfs"]
    sb["boT"] = cfs[:, 0:32]
    sb["bbT"] = cfs[:, 32:64]
    sb["ones1f"] = cfs[0:1, 64:192]
    sb["ones4f"] = cfs[0:4, 64:65]
    sb["ones14"] = cfs[0:1, 64:68]
    sb["E432"] = cfs[0:4, 192:224]
    sb["rbp4"] = cfs[0:4, 224:225]
    sb["prp4"] = cfs[0:4, 225:226]
    sb["epsb"] = cfs[0:1, 226:227]
    return sb


def alloc_psums(nc, sb):
    sb["scA"] = nc.alloc_psum_tensor("scA", [128, 1024], F32).ap()
    sb["scB"] = nc.alloc_psum_tensor("scB", [128, 1024], F32).ap()
    sb["scC"] = nc.alloc_psum_tensor("scC", [128, 1024], F32).ap()
    sb["pcP"] = nc.alloc_psum_tensor("pcP", [128, 1024], F32).ap()
    sb["ps512"] = [t[:, c:c + 512]
                   for t in (sb["scA"], sb["scB"], sb["scC"], sb["pcP"])
                   for c in (0, 512)]
    sb["ps512k"] = ["psA0", "psA1", "psB0", "psB1", "psC0", "psC1", "psP0", "psP1"]
    sb["pctx"] = [(f"pctx{k}", sb["pcP"][:, k * 256: k * 256 + 129]) for k in range(4)]
    sb["pso"] = [("psoA", sb["scC"][:, 0:512]), ("psoB", sb["scC"][:, 512:1024])]
    sb["meanps"] = sb["scA"][0:1, 0:1024]
    sb["varps"] = sb["scB"][0:1, 0:1024]
    sb["psbc"] = sb["scB"]
    sb["psz"] = sb["scA"][0:36, 0:1024]


def emit_rope_evac(p, sb, ps_key, ps_ap, dst_ap, dst_keys, bias_ap, half, par,
                   tag, war_vstg=False):
    e1 = sb["ropews"]["e1"][par]
    sh = sb["ropews"]["sh"][par]
    c_ap = sb["ropeC"][:, half * 512: half * 512 + 512]
    s_ap = sb["ropeS"][:, half * 512: half * 512 + 512]
    ek, shk = f"re1_{par}", f"rsh_{par}"
    vw = ["vstg"] if war_vstg else []
    p.add("act", lambda e: e.activation(e1, ps_ap, AF.Identity, bias=bias_ap),
          reads=[ps_key, "c_b"], writes=[ek] + vw, tag=f"{tag}e1")
    p.add("sp", lambda e: e.dma_start(out=sh[0:64, :], in_=e1[64:128, :]),
          reads=[ek], writes=[shk] + vw, dma=True, tag=f"{tag}sl")
    p.add("sp", lambda e: e.dma_start(out=sh[64:128, :], in_=e1[0:64, :]),
          reads=[ek], writes=[shk] + vw, dma=True, tag=f"{tag}sh")
    p.add("dve", lambda e: e.tensor_mul(sh, sh, s_ap),
          reads=[shk, "c_rope"], writes=[shk], tag=f"{tag}t2")
    p.add("dve", lambda e: e.tensor_mul(e1, e1, c_ap),
          reads=[ek, "c_rope"], writes=[ek], tag=f"{tag}t1")
    p.add("dve", lambda e: e.tensor_add(dst_ap, e1, sh),
          reads=[ek, shk], writes=list(dst_keys), tag=f"{tag}su")


def p2_group_load(p, sb, io, g, eng):
    par = g % 2
    Kg = [sb["KgA"], sb["KgB"]]
    Vg = [sb["VgA"], sb["VgB"]]
    kx = ["wqA"] if g < 2 else []
    vx = (["wqB"] + (["re1_0"] if g == 1 else [])) if g < 2 else []
    for j in range(4):
        p.add(eng, lambda e, g=g, j=j, par=par: e.dma_start(
            out=Kg[par].rearrange("p (b m) -> p b m", m=128)[:, j:32:4, :],
            in_=io["kv_gK"][j * 1024 + g * 128: j * 1024 + (g + 1) * 128, :]
                .rearrange("p (k m) -> p k m", m=128)),
            reads=[f"kg{par}", "kvgK"],
            writes=[f"kg{par}"] + (kx if j == 0 else []), dma=True, tag=f"ldK{g}_{j}")
        p.add(eng, lambda e, g=g, j=j, par=par: e.dma_start(
            out=Vg[par].rearrange("p (b c) -> p b c", c=129)[:, j:32:4, :],
            in_=io["kv_gV"][j * 1024:(j + 1) * 1024, g * 129:(g + 1) * 129]
                .rearrange("(k p) c -> p k c", p=128)),
            reads=[f"vg{par}", "kvgV"],
            writes=[f"vg{par}"] + (vx if j == 0 else []), dma=True, tag=f"ldV{g}_{j}")


def build_p1(p, io, sb, mid_cb=None):
    wb = [sb["wbufA"], sb["wbufB"]]
    ps = sb["ps512"]
    psk = sb["ps512k"]
    p.add("pool", lambda e: e.dma_start(
        out=wb[0][:].rearrange("p (c m) -> p c m", c=32),
        in_=io["wv"][:, 0:256].rearrange("(c p) m -> p c m", p=128)),
        reads=["wb0"], writes=["wb0"], dma=True, tag="ldwv0")
    for grp in range(4):
        p.add("pool", lambda e, grp=grp: e.dma_start(
            out=sb["hT"][:, grp * 4096:(grp + 1) * 4096].rearrange(
                "p (c m) -> p c m", c=4),
            in_=io["hT"][grp * 512:(grp + 1) * 512, :].rearrange(
                "(c p) m -> p c m", p=128)),
            writes=[f"hT{grp}"], dma=True, tag=f"ldh{grp}")
    p.add("sp", lambda e: e.dma_start(out=sb["ropeC"], in_=io["ropeC"]),
          writes=["c_rope"], dma=True, tag="ld_rc")
    p.add("sp", lambda e: e.dma_start(out=sb["ropeS"], in_=io["ropeS"]),
          writes=["c_rope"], dma=True, tag="ld_rs")
    p.add("sp", lambda e: e.dma_start(out=sb["bv"], in_=io["bv_rep"]),
          writes=["c_bv"], dma=True, tag="ld_bv")
    p.add("sp", lambda e: e.dma_start(out=sb["bqT"], in_=io["bqT"]),
          writes=["c_b"], dma=True, tag="ld_bq")
    p.add("sp", lambda e: e.dma_start(out=sb["bkT"], in_=io["bkT"]),
          writes=["c_b"], dma=True, tag="ld_bk")
    for grp in range(4, 8):
        par = grp % 2
        stg, skey = sb["hstg"][par], f"hstg{par}"
        p.add("sp", lambda e, grp=grp, stg=stg: e.dma_start(
            out=stg[:].rearrange("p (c m) -> p c m", c=4),
            in_=io["hT"][grp * 512:(grp + 1) * 512, :].rearrange(
                "(c p) m -> p c m", p=128)),
            reads=[skey], writes=[skey], dma=True, tag=f"ldhs{grp}")
        p.add("act", lambda e, grp=grp, stg=stg: e.activation(
            sb["hT"][:, grp * 4096:(grp + 1) * 4096], stg[:], AF.Identity),
            reads=[skey], writes=[f"hT{grp}"], tag=f"hcast{grp}")
    vst = sb["v_stage"]
    p.add("dve", lambda e: e.memset(
        vst.rearrange("p (t b c) -> p t b c", b=8, c=129)[:, :, :, 128:129], 1.0),
        writes=["vstg"], tag="ones_v")
    for q in range(4):
        par = q % 2
        if q > 0:
            p.add("pool", lambda e, q=q, par=par: e.dma_start(
                out=wb[par][:].rearrange("p (c m) -> p c m", c=32),
                in_=io["wv"][:, q * 256:(q + 1) * 256].rearrange("(c p) m -> p c m", p=128)),
                reads=[f"wb{par}"], writes=[f"wb{par}"], dma=True, tag=f"ldwv{q}")
        for tt in range(8):
            pp = (q * 8 + tt) % 8
            for c in range(32):
                p.add("pe", lambda e, par=par, tt=tt, c=c, pp=pp: e.matmul(
                    ps[pp][:, 0:256],
                    sb["hT"][:, c * 1024 + tt * 128: c * 1024 + tt * 128 + 128],
                    wb[par][:, c * 256:(c + 1) * 256],
                    start=(c == 0), stop=(c == 31)),
                    reads=[f"hT{c // 4}", f"wb{par}"], writes=[psk[pp]],
                    tag=f"vmm{q}_{tt}_{c}")
            dst = vst[:, tt * 1032:(tt + 1) * 1032].rearrange(
                "p (b c) -> p b c", c=129)[:, 2 * q:2 * q + 2, 0:128]
            p.add("dve", lambda e, q=q, pp=pp, dst=dst: e.tensor_add(
                dst, ps[pp][:, 0:256].rearrange("p (b c) -> p b c", c=128),
                sb["bv"][:, q * 256:(q + 1) * 256].rearrange("p (b c) -> p b c", c=128)),
                reads=[psk[pp], "c_bv"], writes=["vstg"], tag=f"vev{q}_{tt}")
    for gp in range(4):
        extra = ["hstg0"] if gp == 2 else (["hstg1"] if gp == 3 else [])
        p.add("pool", lambda e, gp=gp: e.dma_start(
            out=sb["wkR2"][gp][:].rearrange("p (c m) -> p c m", c=32),
            in_=io["wk"][:, gp * 256:(gp + 1) * 256].rearrange("(c p) m -> p c m", p=128)),
            writes=[f"wkR2_{gp}"] + extra, dma=True, tag=f"ldwk{gp}")
    p.add("sp", lambda e: e.dma_start(
        out=io["kv_ownV"][:].rearrange("(t p) c -> p t c", p=128),
        in_=vst.rearrange("p (t c) -> p t c", t=8)),
        reads=["vstg"], writes=["kvownV"], dma=True, tag="st_v")
    p.add("pool", lambda e: e.collective_compute(
        "AllGather", ALU.bypass, replica_groups=[[0, 1, 2, 3], [4, 5, 6, 7]],
        ins=[io["kv_ownV"].opt()], outs=[io["kv_gV"].opt()]),
        reads=["kvownV"], writes=["kvgV"], tag="agV")
    for gp in range(4):
        for g in (2 * gp, 2 * gp + 1):
            for half in range(2):
                pp = (g * 2 + half) % 8
                for c in range(32):
                    p.add("pe", lambda e, gp=gp, g=g, half=half, c=c, pp=pp: e.matmul(
                        ps[pp][:],
                        sb["wkR2"][gp][:, c * 256 + (g % 2) * 128: c * 256 + (g % 2) * 128 + 128],
                        sb["hT"][:, c * 1024 + half * 512: c * 1024 + half * 512 + 512],
                        start=(c == 0), stop=(c == 31)),
                        reads=[f"hT{c // 4}", f"wkR2_{gp}"], writes=[psk[pp]],
                        tag=f"kmm{g}_{half}_{c}")
                emit_rope_evac(p, sb, psk[pp], ps[pp][:],
                               sb["kstage"][g % 2][:, half * 512: half * 512 + 512],
                               [f"kst{g % 2}"], sb["bkT"][:, g:g + 1], half,
                               (g * 2 + half) % 4, f"k{g}{half}", war_vstg=(g < 2))
            p.add("sp", lambda e, g=g: e.dma_start(
                out=io["kv_ownK"][g * 128:(g + 1) * 128, :], in_=sb["kstage"][g % 2][:]),
                reads=[f"kst{g % 2}"], writes=["kvownK"], dma=True, tag=f"stk{g}")
    QROT = [0, 1, 2, 3, 0, 1, 2, 0, 1, 2, 0, 1, 0, 1, 0, 1]
    wqb = [sb["wbufA"], sb["wbufB"], sb["wqC"], sb["wqD"]]
    wqk = ["wqA", "wqB", "wqC", "wqD"]
    wqx = {0: ["wb0"], 1: ["wb1"], 2: ["wkR2_3"], 3: ["wkR2_2"]}
    for hp in range(16):
        if hp == 4 and mid_cb is not None:
            mid_cb()
        r = QROT[hp]
        p.add("pool", lambda e, hp=hp, r=r: e.dma_start(
            out=wqb[r][:].rearrange("p (c m) -> p c m", c=32),
            in_=io["wq"][:, hp * 256:(hp + 1) * 256].rearrange("(c p) m -> p c m", p=128)),
            writes=[wqk[r]] + wqx[r], dma=True, tag=f"ldwq{hp}")
        for h in (2 * hp, 2 * hp + 1):
            for half in range(2):
                pp = (h * 2 + half) % 8
                for c in range(32):
                    p.add("pe", lambda e, r=r, half=half, c=c, pp=pp, h=h: e.matmul(
                        ps[pp][:],
                        wqb[r][:, c * 256 + (h % 2) * 128: c * 256 + (h % 2) * 128 + 128],
                        sb["hT"][:, c * 1024 + half * 512: c * 1024 + half * 512 + 512],
                        start=(c == 0), stop=(c == 31)),
                        reads=[f"hT{c // 4}", wqk[r]], writes=[psk[pp]],
                        tag=f"qmm{h}_{half}_{c}")
                emit_rope_evac(p, sb, psk[pp], ps[pp][:],
                               sb["q"][:, h * 1024 + half * 512: h * 1024 + half * 512 + 512],
                               [f"q_{h}_{half * 4 + kk}" for kk in range(4)]
                               + [f"wkR2_{h // 8}"],
                               sb["bqT"][:, h:h + 1], half, (h * 2 + half) % 4,
                               f"q{h}{half}")
    p2_group_load(p, sb, io, 0, "pool")


def build_p2(p, io, sb):
    Kg = [sb["KgA"], sb["KgB"]]
    Vg = [sb["VgA"], sb["VgB"]]
    e_sb = [sb["eA"], sb["eB"], sb["eC"], sb["eD"]]
    sc = [sb["scA"], sb["scB"], sb["scC"]]
    sck = ["scA", "scB", "scC"]
    pctx = sb["pctx"]
    q = sb["q"]

    for g in range(8):
        par = g % 2
        if 1 <= g + 1 < 8:
            p2_group_load(p, sb, io, g + 1, "sp")
        for h4 in range(4):
            h = 4 * g + h4
            for s in range(2):
                nb = 16 if s == 0 else 32
                hi = 4 * s + 3
                pend = []
                for pr in range(nb // 2):
                    b0 = 2 * pr
                    cs = pr % 3
                    ce = pr % 4
                    lo = (b0 // 4) if s == 0 else max(4, b0 // 4)
                    ncols = (hi - lo + 1) * 128
                    qkeys = [f"q_{h}_{k}" for k in range(lo, hi + 1)]
                    for half in range(2):
                        b = b0 + half
                        p.add("pe", lambda e, b=b, lo=lo, ncols=ncols, half=half,
                              cs=cs, h=h, par=par: e.matmul(
                            sc[cs][:, half * 512: half * 512 + ncols],
                            Kg[par][:, b * 128:(b + 1) * 128],
                            q[:, h * 1024 + lo * 128: h * 1024 + lo * 128 + ncols],
                            start=True, stop=True),
                            reads=[f"kg{par}"] + qkeys, writes=[sck[cs]],
                            tag=f"sc{h}_{s}_{b}")
                    src = sc[cs].rearrange("p (x c) -> p x c", x=2)[:, :, 0:ncols]
                    dst = e_sb[ce].rearrange("p (x c) -> p x c", x=2)[:, :, 0:ncols]
                    p.add("act", lambda e, src=src, dst=dst: e.activation(
                        dst, src, AF.Exp, scale=SCALE),
                        reads=[sck[cs]], writes=[f"e{ce}"], tag=f"ex{h}_{s}_{b0}")
                    if (s == 0) or (b0 >= 16):
                        for half in range(2):
                            r = (b0 + half) % 4
                            p.add("dve", lambda e, ce=ce, half=half, r=r: e.tensor_mul(
                                e_sb[ce][:, half * 512: half * 512 + 128],
                                e_sb[ce][:, half * 512: half * 512 + 128],
                                sb["masks"][:, r * 128:(r + 1) * 128]),
                                reads=[f"e{ce}", "c_mi"], writes=[f"e{ce}"],
                                tag=f"mk{h}_{s}_{b0 + half}")
                    if len(pend) == 2:
                        pend.pop(0)()
                    def make_ctx(ce=ce, lo=lo, b0=b0, s=s, par=par, h=h, hi=hi):
                        def emit_ctx():
                            for half in range(2):
                                b = b0 + half
                                for k in range(lo, hi + 1):
                                    kp = k - 4 * s
                                    kk, kap = pctx[kp]
                                    st = (b == 0 and kp % 2 == 0)
                                    ws = [kk] + ([pctx[kp + 1][0]] if st else [])
                                    p.add("pe", lambda e, ce=ce, half=half, k=k,
                                          lo=lo, b=b, kap=kap, par=par, st=st: e.matmul(
                                        kap,
                                        e_sb[ce][:, half * 512 + (k - lo) * 128:
                                                  half * 512 + (k - lo + 1) * 128],
                                        Vg[par][:, b * 129:(b + 1) * 129],
                                        start=st, stop=(b == 4 * k + 3),
                                        skip_group_check=True),
                                        reads=[f"e{ce}", f"vg{par}"], writes=ws,
                                        tag=f"cx{h}_{s}_{b}_{k}")
                        return emit_ctx
                    pend.append(make_ctx())
                for f in pend:
                    f()
                for k in range(4 * s, 4 * s + 4):
                    kk, kap = pctx[k - 4 * s]
                    p.add("dve", lambda e, kap=kap, k=k: e.reciprocal(
                        sb["rcp"][:, k:k + 1], kap[:, 128:129]),
                        reads=[kk], writes=["rcp"], tag=f"rc{h}_{k}")
                    p.add("act", lambda e, kap=kap, k=k, h=h: e.activation(
                        sb["ctxnat"][h][k], kap[:, 0:128], AF.Identity,
                        scale=sb["rcp"][:, k:k + 1]),
                        reads=[kk, "rcp"], writes=[f"q_{h}_{k}"], tag=f"cn{h}_{k}")
                    p.add("sp", lambda e, h=h, k=k: e.dma_start(
                        out=sb["ctxT"][:, h * 1024 + k * 128: h * 1024 + (k + 1) * 128],
                        in_=sb["ctxnat"][h][k], transpose=True),
                        reads=[f"q_{h}_{k}"], writes=["ctxT"], dma=True,
                        tag=f"tx{h}_{k}")


def build_p3t(p, io, sb):
    wo_sb = sb["wo_sb"]
    xT = sb["xTsl"]
    r1ev = sb["r1ev"]
    r1bf = sb["r1bf"]
    d2T = sb["d2T"]
    pso = sb["pso"]

    def mm_stage(which, rhs, rhs_key_fn, w_src, bias_ap, x_src, x_key, r_dst,
                 r_key, lora, bf16_to_R2=False, norm_x=False):
        for oc in range(32):
            par = oc % 2
            p.add("pool", lambda e, oc=oc, par=par: e.dma_start(
                out=wo_sb[par][:].rearrange("p (c m) -> p c m", c=32),
                in_=w_src[:, oc * 128:(oc + 1) * 128].rearrange("(c p) m -> p c m", p=128)),
                reads=[f"wo{par}"], writes=[f"wo{par}"], dma=True, tag=f"{which}w{oc}")
            extra = ["vg0", "vg1", "e0", "e1"] if (which == "A" and oc < 2) else []
            p.add("sp", lambda e, oc=oc, par=par: e.dma_start(
                out=xT[par][:], in_=x_src[oc * 128:(oc + 1) * 128, :]),
                reads=[f"xT{par}"] + ([f"{x_key}_{oc}"] if x_key else []) + extra,
                writes=[f"xT{par}"], dma=True, tag=f"{which}x{oc}")
            if norm_x:
                p.add("dve", lambda e, par=par: e.tensor_sub(
                    xT[par][:], xT[par][:], sb["mrep"]),
                    reads=[f"xT{par}", "mrep"], writes=[f"xT{par}"], tag=f"{which}nx{oc}")
                p.add("dve", lambda e, par=par: e.tensor_mul(
                    xT[par][:], xT[par][:], sb["rrep"]),
                    reads=[f"xT{par}", "rrep"], writes=[f"xT{par}"], tag=f"{which}ny{oc}")
            for half in range(2):
                pk, pap = pso[half]
                for c in range(32):
                    pw = [pk] + (["scC"] if (which == "A" and oc == 0 and c == 0)
                                 else [])
                    p.add("pe", lambda e, par=par, half=half, c=c, pap=pap: e.matmul(
                        pap,
                        wo_sb[par][:, c * 128:(c + 1) * 128],
                        rhs[:, c * 1024 + half * 512: c * 1024 + half * 512 + 512],
                        start=(c == 0), stop=(c == 31 and not lora)),
                        reads=[rhs_key_fn(c), f"wo{par}"], writes=pw,
                        tag=f"{which}m{oc}_{half}_{c}")
                if lora:
                    p.add("pe", lambda e, oc=oc, half=half, pap=pap: e.matmul(
                        pap, sb["bcat_sb"][0:32, oc * 128:(oc + 1) * 128],
                        sb["zwT"][0:32, half * 512: half * 512 + 512],
                        start=False, stop=True),
                        reads=["c_bc", "zwT"], writes=[pk], tag=f"lr{oc}_{half}")
                p.add("dve", lambda e, oc=oc, par=par, half=half, pap=pap:
                    e.scalar_tensor_tensor(
                        r1ev[par][:, half * 512: half * 512 + 512], pap,
                        bias_ap[:, oc:oc + 1],
                        xT[par][:, half * 512: half * 512 + 512],
                        ALU.add, ALU.add),
                    reads=[pso[half][0], f"xT{par}", "c_f"], writes=[f"r1e{par}"],
                    tag=f"{which}e{oc}_{half}")
            p.add("sp", lambda e, oc=oc, par=par: e.dma_start(
                out=r_dst[oc * 128:(oc + 1) * 128, :], in_=r1ev[par][:]),
                reads=[f"r1e{par}"], writes=[f"{r_key}_{oc}"], dma=True,
                tag=f"{which}s{oc}")
            if bf16_to_R2:
                bdst, bkey = sb["x1T"][:, oc * 1024:(oc + 1) * 1024], f"x1T_{oc}"
            else:
                bdst, bkey = r1bf[par][:], f"r1b{par}"
            p.add("act", lambda e, bdst=bdst, par=par: e.activation(
                bdst, r1ev[par][:], AF.Identity),
                reads=[f"r1e{par}"], writes=[bkey], tag=f"{which}b{oc}")
            p.add("act", lambda e, par=par: e.activation(
                d2T[par][:], r1ev[par][:], AF.Square),
                reads=[f"r1e{par}"], writes=[f"d2{par}"], tag=f"{which}d{oc}")
            for half in range(2):
                bsrc = bdst if bf16_to_R2 else r1bf[par][:]
                p.add("pe", lambda e, oc=oc, half=half, bsrc=bsrc: e.matmul(
                    sb["meanps"][:, half * 512: half * 512 + 512],
                    sb["onesc"][:, 0:1],
                    bsrc[:, half * 512: half * 512 + 512],
                    start=(oc == 0), stop=(oc == 31)),
                    reads=[bkey, "c_mi"], writes=["scA"],
                    tag=f"{which}sm{oc}_{half}")
                p.add("pe", lambda e, oc=oc, par=par, half=half: e.matmul(
                    sb["varps"][:, half * 512: half * 512 + 512],
                    sb["onesc"][:, 0:1],
                    d2T[par][:, half * 512: half * 512 + 512],
                    start=(oc == 0), stop=(oc == 31)),
                    reads=[f"d2{par}", "c_mi"], writes=["scB"],
                    tag=f"{which}sv{oc}_{half}")

    def ln_stats_apply(which, r_src, r_key, dst_bf16, dst_bf16_key, dst_f32_dram,
                       dst_f32_key, out_flag):
        lnA, lnB, lnC = sb["lnA"], sb["lnB"], sb["alw"][0:1, :]
        p.add("dve", lambda e: e.tensor_scalar(
            lnA, sb["meanps"], 1.0 / 4096.0, None, ALU.mult),
            reads=["scA"], writes=["lnm"], tag=f"{which}m")
        p.add("dve", lambda e: e.tensor_mul(lnB, lnA, lnA),
            reads=["lnm"], writes=["lnm2"], tag=f"{which}m2")
        p.add("dve", lambda e: e.scalar_tensor_tensor(
            lnB, sb["varps"], 1.0 / 4096.0, lnB, ALU.mult, ALU.subtract),
            reads=["scB", "lnm2"], writes=["lnv"], tag=f"{which}v")
        p.add("act", lambda e: e.activation(
            lnC, lnB, AF.Ln, bias=sb["epsb"]),
            reads=["lnv", "c_f", "alg"], writes=["lnl"], tag=f"{which}l")
        p.add("act", lambda e: e.activation(
            lnB, lnC, AF.Exp, scale=-0.5),
            reads=["lnl"], writes=["lnr"], tag=f"{which}r")
        for half in range(2):
            p.add("pe", lambda e, half=half: e.matmul(
                sb["psbc"][:, half * 512: half * 512 + 512],
                sb["ones1f"], lnA[:, half * 512: half * 512 + 512],
                start=True, stop=True),
                reads=["lnm", "c_f"], writes=["scB"], tag=f"{which}bm{half}")
        p.add("dve", lambda e: e.tensor_copy(sb["mrep"], sb["psbc"][:, 0:1024]),
              reads=["scB"], writes=["mrep"], tag=f"{which}cm")
        for half in range(2):
            p.add("pe", lambda e, half=half: e.matmul(
                sb["psbc"][:, half * 512: half * 512 + 512],
                sb["ones1f"], lnB[:, half * 512: half * 512 + 512],
                start=True, stop=True),
                reads=["lnr", "c_f", "mrep"], writes=["scB"], tag=f"{which}br{half}")
        p.add("dve", lambda e: e.tensor_copy(sb["rrep"], sb["psbc"][:, 0:1024]),
              reads=["scB"], writes=["rrep"], tag=f"{which}cr")
        if dst_f32_dram is None:
            for oc in range(32):
                ch = sb["x1T"][:, oc * 1024:(oc + 1) * 1024]
                p.add("dve", lambda e, ch=ch: e.tensor_sub(ch, ch, sb["mrep"]),
                    reads=[f"x1T_{oc}", "mrep"], writes=[f"x1T_{oc}"],
                    tag=f"{which}as{oc}")
                p.add("dve", lambda e, ch=ch: e.tensor_mul(ch, ch, sb["rrep"]),
                    reads=[f"x1T_{oc}", "rrep"], writes=[f"x1T_{oc}"],
                    tag=f"{which}am{oc}")
            return
        r1a = sb["r1ap"]
        rk = sb["r1ap_keys"]
        for oc in range(32):
            par = oc % 8
            buf = r1a[par]
            p.add("sp", lambda e, oc=oc, buf=buf: e.dma_start(
                out=buf[:], in_=r_src[oc * 128:(oc + 1) * 128, :]),
                reads=[rk[par], f"{r_key}_{oc}"], writes=[rk[par]], dma=True,
                tag=f"{which}al{oc}")
            p.add("dve", lambda e, buf=buf: e.tensor_sub(buf[:], buf[:], sb["mrep"]),
                reads=[rk[par], "mrep"], writes=[rk[par]], tag=f"{which}as{oc}")
            p.add("dve", lambda e, buf=buf: e.tensor_mul(buf[:], buf[:], sb["rrep"]),
                reads=[rk[par], "rrep"], writes=[rk[par]], tag=f"{which}am{oc}")
            p.add("pool" if oc % 2 == 0 else "sp",
                lambda e, oc=oc, buf=buf: e.dma_start(
                    out=dst_f32_dram[oc * 128:(oc + 1) * 128, :], in_=buf[:]),
                reads=[rk[par]], writes=[f"{dst_f32_key}_{oc}"], dma=True,
                out=out_flag, tag=f"{which}af{oc}")

    mm_stage("A", sb["ctxT"], lambda c: "ctxT", io["wo"], sb["boT"], io["hT"],
             None, io["r1_dram"], "r1d", False, bf16_to_R2=True)
    p.fence()
    p.add("pool", lambda e: e.dma_start(out=sb["bcat_sb"], in_=io["bcat"]),
          writes=["c_bc"], dma=True, tag="ld_bc")
    p.add("pool", lambda e: e.dma_start(
        out=sb["asm"][:].rearrange("p (c m) -> p c m", c=32),
        in_=io["asmall"][:].rearrange("(c p) m -> p c m", p=128)),
        writes=["c_as"], dma=True, tag="ld_as")
    ln_stats_apply("L1", io["r1_dram"], "r1d", sb["x1T"],
                   lambda oc: f"x1T_{oc}", None, None, False)
    for half in range(2):
        for c in range(32):
            p.add("pe", lambda e, c=c, half=half: e.matmul(
                sb["psz"][:, half * 512: half * 512 + 512],
                sb["asm"][:, c * 36:(c + 1) * 36],
                sb["x1T"][:, c * 1024 + half * 512: c * 1024 + half * 512 + 512],
                start=(c == 0), stop=(c == 31)),
                reads=["c_as", f"x1T_{c}"], writes=["scA"], tag=f"z{half}_{c}")
    p.add("dve", lambda e: e.tensor_copy(sb["ztf"][0:36, 0:1024],
                                         sb["psz"][0:36, 0:1024]),
          reads=["scA"], writes=["ztf"], tag="zev")
    p.add("sp", lambda e: e.dma_start(out=sb["alw"][0:4, 0:1024],
                                      in_=sb["ztf"][32:36, 0:1024]),
          reads=["ztf"], writes=["alg"], dma=True, tag="alshift")
    al = sb["alw"]
    p.add("dve", lambda e: e.tensor_scalar(
        al[0:4, :], al[0:4, :], sb["rbp4"], None, ALU.add),
        reads=["alg", "c_f"], writes=["alg"], tag="al_rb")
    p.add("act", lambda e: e.activation(al[0:4, :], al[0:4, :], AF.Exp),
          reads=["alg"], writes=["alg"], tag="al_exp")
    p.add("dve", lambda e: e.tensor_scalar(
        al[0:4, :], al[0:4, :], sb["prp4"], None, ALU.mult),
        reads=["alg", "c_rp"], writes=["alg"], tag="al_pr")
    for half in range(2):
        p.add("pe", lambda e, half=half: e.matmul(
            sb["scA"][0:1, half * 512: half * 512 + 512],
            sb["ones4f"], al[0:4, half * 512: half * 512 + 512],
            start=True, stop=True),
            reads=["alg", "c_f"], writes=["scA"], tag=f"alden{half}")
    p.add("act", lambda e: e.activation(sb["lnB"], sb["scA"][0:1, 0:1024], AF.Ln),
          reads=["scA", "lnr"], writes=["alrcp"], tag="alrc1")
    p.add("act", lambda e: e.activation(sb["lnB"], sb["lnB"], AF.Exp, scale=-1.0),
          reads=["alrcp"], writes=["alrcp"], tag="alrc2")
    for half in range(2):
        p.add("pe", lambda e, half=half: e.matmul(
            sb["scB"][0:4, half * 512: half * 512 + 512],
            sb["ones14"], sb["lnB"][:, half * 512: half * 512 + 512],
            start=True, stop=True),
            reads=["alrcp", "c_f"], writes=["scB"], tag=f"albc{half}")
    p.add("dve", lambda e: e.tensor_mul(
        al[0:4, :], al[0:4, :], sb["scB"][0:4, 0:1024]),
        reads=["alg", "scB"], writes=["alg"], tag="al_n")
    for half in range(2):
        p.add("pe", lambda e, half=half: e.matmul(
            sb["scA"][0:32, half * 512: half * 512 + 512],
            sb["E432"], al[0:4, half * 512: half * 512 + 512],
            start=True, stop=True),
            reads=["alg", "c_f"], writes=["scA"], tag=f"alx{half}")
    p.add("dve", lambda e: e.tensor_copy(sb["al32"][0:32, :],
                                         sb["scA"][0:32, 0:1024]),
          reads=["scA"], writes=["al32"], tag="alxc")
    p.add("dve", lambda e: e.scalar_tensor_tensor(
        sb["zwT"][0:32, :], sb["ztf"][0:32, :], 2.0, sb["al32"][0:32, :],
        ALU.mult, ALU.mult),
        reads=["ztf", "al32"], writes=["zwT"], tag="zw")
    mm_stage("B", sb["x1T"], lambda c: f"x1T_{c}", io["wbT"], sb["bbT"],
             io["r1_dram"], "r1d", io["r2_dram"], "r2d", True, norm_x=True)
    ln_stats_apply("L2", io["r2_dram"], "r2d", None, None, io["outT"], "outd", True)


def build_fused():
    nc = bass.Bass("TRN2", target_bir_lowering=False)
    io = {}
    ein = lambda n, s_, d: nc.dram_tensor(n, s_, d, kind="ExternalInput").ap()
    io["hT"] = ein("hT", [4096, 1024], F32)
    io["wq"] = ein("wq", [4096, 4096], F32)
    io["wk"] = ein("wk", [4096, 1024], F32)
    io["wv"] = ein("wv", [4096, 1024], F32)
    io["ropeC"] = ein("ropeC", [128, 1024], F32)
    io["ropeS"] = ein("ropeS", [128, 1024], F32)
    io["bqT"] = ein("bqT", [128, 32], F32)
    io["bkT"] = ein("bkT", [128, 8], F32)
    io["bv_rep"] = ein("bv_rep", [128, 1024], F32)
    io["wo"] = ein("wo", [4096, 4096], F32)
    io["wbT"] = ein("wbT", [4096, 4096], F32)
    io["masks"] = ein("masks", [128, 512], BF16)
    io["identc"] = ein("identc", [128, 256], BF16)
    io["cf32"] = ein("cf32", [128, 256], F32)
    io["asmall"] = ein("asmall", [4096, 36], F32)
    io["bcat"] = ein("bcat", [32, 4096], F32)
    io["outT"] = nc.dram_tensor("outT", [4096, 1024], F32, kind="ExternalOutput").ap()
    io["kv_ownK"] = nc.dram_tensor("kv_ownK", [1024, 1024], BF16, kind="Internal").ap()
    io["kv_ownV"] = nc.dram_tensor("kv_ownV", [1024, 1032], BF16, kind="Internal").ap()
    io["kv_gK"] = nc.dram_tensor("kv_gK", [4096, 1024], BF16, kind="Internal").ap()
    io["kv_gV"] = nc.dram_tensor("kv_gV", [4096, 1032], BF16, kind="Internal").ap()
    io["r1_dram"] = nc.dram_tensor("r1_dram", [4096, 1024], F32, kind="Internal").ap()
    io["r2_dram"] = nc.dram_tensor("r2_dram", [4096, 1024], F32, kind="Internal").ap()
    sb = alloc_bufs(nc)
    alloc_psums(nc, sb)
    sb["hT"] = sb["ctxT"]
    p = Prog(nc)
    p.add("sp", lambda e: e.dma_start(out=sb["masks"], in_=io["masks"]),
          writes=["c_mi"], dma=True, tag="ld_mk")
    p.add("sp", lambda e: e.dma_start(out=sb["identcv"], in_=io["identc"]),
          writes=["c_mi"], dma=True, tag="ld_id")
    p.add("sp", lambda e: e.dma_start(out=sb["cfs"][:], in_=io["cf32"]),
          writes=["c_f"], dma=True, tag="ld_cf")
    p.add("dve", lambda e: e.tensor_scalar(
        sb["prp4"], sb["prp4"], 1e-12, None, ALU.add),
        reads=["c_f"], writes=["c_rp"], tag="preps")

    def collective():
        p.add("pool", lambda e: e.collective_compute(
            "AllGather", ALU.bypass, replica_groups=[[0, 1, 2, 3], [4, 5, 6, 7]],
            ins=[io["kv_ownK"].opt()], outs=[io["kv_gK"].opt()]),
            reads=["kvownK", "kvgV"], writes=["kvgK"], tag="agK")

    build_p1(p, io, sb, mid_cb=collective)
    p.fence()
    build_p2(p, io, sb)
    build_p3t(p, io, sb)
    p.emit()
    return nc


def stage_core_inputs(inputs, c):
    j, bb = c % 4, c // 4
    x = np.asarray(inputs["L0_attn_hidden_states"][bb])
    rows = np.concatenate([np.arange((4 * k + j) * 128, (4 * k + j + 1) * 128)
                           for k in range(8)])
    hT = np.ascontiguousarray(x[rows].T)
    half = D // 2
    inv = 1.0 / (10000.0 ** (np.arange(half, dtype=np.float64) / half))
    ang = rows[:, None].astype(np.float64) * inv[None, :]
    cosT = np.cos(ang).T.astype(np.float32)
    sinT = np.sin(ang).T.astype(np.float32)
    ropeC = np.ascontiguousarray(np.concatenate([cosT, cosT], 0))
    ropeS = np.ascontiguousarray(np.concatenate([-sinT, sinT], 0))
    tri = (np.arange(128)[:, None] <= np.arange(128)[None, :]).astype(np.float32)
    m = np.zeros((128, 512), np.float32)
    for r in range(4):
        if r < j:
            m[:, r * 128:(r + 1) * 128] = 1.0
        elif r == j:
            m[:, r * 128:(r + 1) * 128] = tri
    return dict(
        hT=hT, rows=rows, bb=bb, ropeC=ropeC, ropeS=ropeS,
        masks=np.ascontiguousarray(m.astype(ml_dtypes.bfloat16)),
        bqT=np.ascontiguousarray(np.asarray(inputs["L0_attn_bq"]).reshape(32, 128).T),
        bkT=np.ascontiguousarray(np.asarray(inputs["L0_attn_bk"]).reshape(8, 128).T),
    )


def kernel(**inputs):
    inputs = {k: np.asarray(v) for k, v in inputs.items()}
    cores = list(range(8))
    act = inputs["L0_ffn_active"].astype(np.int64)
    rep = lambda v: np.ascontiguousarray(np.broadcast_to(
        np.asarray(v, np.float32)[None, :], (128, np.asarray(v).shape[0])))
    identc = np.zeros((128, 256), ml_dtypes.bfloat16)
    identc[:, 0:128] = np.eye(128, dtype=ml_dtypes.bfloat16)
    identc[:, 128:256] = np.ones((128, 128), ml_dtypes.bfloat16)
    cf32 = np.zeros((128, 256), np.float32)
    cf32[:, 0:32] = inputs["L0_attn_bo"].reshape(32, 128).T
    cf32[:, 32:64] = inputs["L0_ffn_b_base"].reshape(32, 128).T
    cf32[:, 64:192] = 1.0
    for ee in range(4):
        cf32[ee, 192 + ee * 8:192 + (ee + 1) * 8] = 1.0
    cf32[0:4, 224] = inputs["L0_ffn_router_b"][act].astype(np.float32)
    cf32[0:4, 225] = inputs["L0_ffn_domain_prior"][act].astype(np.float32)
    cf32[:, 226] = EPS_LN
    shared = dict(
        wq=inputs["L0_attn_Wq"], wk=inputs["L0_attn_Wk"], wv=inputs["L0_attn_Wv"],
        wo=inputs["L0_attn_Wo"],
        wbT=np.ascontiguousarray(inputs["L0_ffn_W_base"].T),
        bv_rep=rep(inputs["L0_attn_bv"]),
        cf32=cf32, identc=identc,
        asmall=np.ascontiguousarray(np.concatenate(
            [inputs["L0_ffn_lora_A"][act].reshape(32, HID).T,
             inputs["L0_ffn_router_W"][act].T], axis=1)),
        bcat=np.ascontiguousarray(
            inputs["L0_ffn_lora_B"][act].transpose(0, 2, 1).reshape(32, HID)),
    )
    staged = [stage_core_inputs(inputs, c) for c in cores]
    nc = build_fused()
    in_maps = [{"hT": st["hT"], "wq": shared["wq"], "wk": shared["wk"],
                "wv": shared["wv"], "ropeC": st["ropeC"], "ropeS": st["ropeS"],
                "bqT": st["bqT"], "bkT": st["bkT"], "bv_rep": shared["bv_rep"],
                "wo": shared["wo"], "wbT": shared["wbT"], "masks": st["masks"],
                "identc": shared["identc"], "cf32": shared["cf32"],
                "asmall": shared["asmall"], "bcat": shared["bcat"]}
               for st in staged]
    res = run_bass_kernel_spmd(nc, in_maps, core_ids=cores)
    out = np.zeros((2, S, HID), np.float32)
    for c in cores:
        out[staged[c]["bb"]][staged[c]["rows"]] = np.asarray(res.results[c]["outT"]).T
    return out


# version 19 (final)
# speedup vs baseline: 1.1049x; 1.0254x over previous
import numpy as np
import ml_dtypes
import concourse.bass as bass
import concourse.mybir as mybir
from concourse.bass_utils import run_bass_kernel_spmd

F32 = mybir.dt.float32
BF16 = mybir.dt.bfloat16
AF = mybir.ActivationFunctionType
ALU = mybir.AluOpType
AX = mybir.AxisListType

H, G, D, HID = 32, 8, 128, 4096
S = 4096
SCALE = float(1.0 / np.sqrt(np.float64(D)))
EPS_LN = 1e-5
QKEYS = [f"q_{h}_{k}" for h in range(H) for k in range(8)]


class Prog:

    ENGS = ("pe", "act", "dve", "pool", "sp")
    HANDLE = {"pe": "tensor", "act": "scalar", "dve": "vector",
              "pool": "gpsimd", "sp": "sync"}

    def __init__(self, nc):
        self.nc = nc
        self.ops = []
        self.fence_marks = []

    def add(self, engine, fn, reads=(), writes=(), dma=False, out=False, tag=""):
        self.ops.append(dict(e=engine, fn=fn, r=tuple(reads), w=tuple(writes),
                             dma=dma, out=out, tag=tag, idx=len(self.ops)))

    def fence(self):
        self.fence_marks.append(len(self.ops))

    def emit(self):
        nc = self.nc
        out_keys = sorted({k for op in self.ops if op["out"] for k in op["w"]})
        if out_keys:
            self.add("sp", lambda e: e.nop(hint="final"), reads=out_keys, tag="final")
        n = len(self.ops)
        deps = [set() for _ in range(n)]
        last_w, readers = {}, {}
        fence_at = {}
        for m in self.fence_marks:
            last = {}
            for op in self.ops[:m]:
                last[op["e"]] = op["idx"]
            fence_at[m] = last
        marks = sorted(self.fence_marks)
        mi = 0
        active = {}
        for op in self.ops:
            i = op["idx"]
            while mi < len(marks) and marks[mi] <= i:
                active = fence_at[marks[mi]]
                mi += 1
            for e, j in active.items():
                if e != op["e"]:
                    deps[i].add(j)
            for r in op["r"]:
                if r in last_w:
                    deps[i].add(last_w[r])
            for w in op["w"]:
                for jj in readers.get(w, ()):
                    deps[i].add(jj)
                if w in last_w:
                    deps[i].add(last_w[w])
            for w in op["w"]:
                last_w[w] = i
                readers[w] = []
            for r in op["r"]:
                readers.setdefault(r, []).append(i)
        for i, op in enumerate(self.ops):
            if op["e"] == "pe" and not op["dma"]:
                deps[i] = {j for j in deps[i]
                           if not (self.ops[j]["e"] == "pe" and not self.ops[j]["dma"])}
            deps[i].discard(i)
        needed = set()
        for i in range(n):
            needed.update(deps[i])
        for op in self.ops:
            if op["dma"]:
                needed.add(op["idx"])
        NSLOT = 8
        semval, cnt, dma_ctr, prewait = {}, {}, {}, {}
        for op in self.ops:
            i = op["idx"]
            if op["dma"]:
                s = dma_ctr.get(op["e"], 0)
                dma_ctr[op["e"]] = s + 1
                key = (op["e"], s % NSLOT)
                prev = cnt.get(key, 0)
                prewait[i] = (key, prev)
                cnt[key] = prev + 16
                semval[i] = (key, cnt[key])
            elif i in needed:
                key = op["e"]
                cnt[key] = cnt.get(key, 0) + 1
                semval[i] = (key, cnt[key])
        pe_next, pend = {}, []
        for op in self.ops:
            if op["e"] == "pe" and not op["dma"]:
                pend.append(op["idx"])
                if op["idx"] in semval:
                    for j in pend:
                        pe_next[j] = op["idx"]
                    pend = []
        sems, ctxs = {}, []
        sem_keys = [e for e in self.ENGS] + \
                   [(e, s) for e in dma_ctr for s in range(NSLOT)]
        for kk in sem_keys:
            nm = kk if isinstance(kk, str) else f"{kk[0]}_d{kk[1]}"
            cm = nc.semaphore(f"sem_{nm}")
            sems[kk] = cm.__enter__()
            ctxs.append(cm)
        waited = {e: {} for e in self.ENGS}
        bad = []

        block_cm = nc.Block()
        block = block_cm.__enter__()

        def make_body(ename):
            def body(eng):
                for op in self.ops:
                    if op["e"] != ename:
                        continue
                    i = op["idx"]
                    for j in sorted(deps[i]):
                        if j not in semval:
                            j2 = pe_next.get(j)
                            if j2 is None or j2 >= i:
                                if self.ops[j]["e"] == "pe" and ename != "pe" and j2 is None:
                                    bad.append((i, j, op["tag"]))
                                continue
                            j = j2
                        key, v = semval[j]
                        if waited[ename].get(key, 0) < v:
                            eng.wait_ge(sems[key], v)
                            waited[ename][key] = v
                    if i in prewait:
                        key, v = prewait[i]
                        if v > 0 and waited[ename].get(key, 0) < v:
                            eng.wait_ge(sems[key], v)
                            waited[ename][key] = v
                    ins = op["fn"](eng)
                    if i in semval:
                        assert ins is not None, f"op {i} ({op['tag']}) returned None"
                        key, _ = semval[i]
                        ins.then_inc(sems[key], 16 if op["dma"] else 1)
            return body

        for e in self.ENGS:
            if any(op["e"] == e for op in self.ops):
                getattr(block, self.HANDLE[e])(make_body(e))
        block_cm.__exit__(None, None, None)
        for cm in reversed(ctxs):
            cm.__exit__(None, None, None)
        assert not bad, f"unresolvable PE deps: {bad[:4]}"


def alloc_bufs(nc):
    sb = {}
    R1 = nc.alloc_sbuf_tensor("R1", [128, 32768], BF16).ap()
    R2 = nc.alloc_sbuf_tensor("R2", [128, 32768], BF16).ap()
    SCR = nc.alloc_sbuf_tensor("SCR", [128, 28672], BF16).ap()
    CP = nc.alloc_sbuf_tensor("CP", [128, 9600], BF16).ap()
    sb["zwT"] = nc.alloc_sbuf_tensor("zwT", [32, 1024], BF16).ap()
    sb["rcp"] = nc.alloc_sbuf_tensor("rcp", [128, 8], F32).ap()
    sb["cfs"] = nc.alloc_sbuf_tensor("cfs", [128, 256], F32).ap()
    scr32 = SCR.bitcast(F32)
    sb["wbufA"] = SCR[:, 0:8192]
    sb["wbufB"] = SCR[:, 8192:16384]
    sb["v_stage"] = SCR[:, 16384:24640]
    sb["kstage"] = [SCR[:, 24640:25664], SCR[:, 25664:26688]]
    rw = lambda i: scr32[:, 8192 + i * 512: 8192 + (i + 1) * 512]
    sb["ropews"] = {"e1": [rw(0), rw(1), rw(2), rw(3)],
                    "sh": [rw(4), rw(5), rw(6), rw(7)]}
    sb["KgA"], sb["KgB"] = SCR[:, 0:4096], SCR[:, 4096:8192]
    sb["VgA"], sb["VgB"] = SCR[:, 8192:12320], SCR[:, 12320:16448]
    sb["eA"], sb["eB"] = SCR[:, 16448:17472], SCR[:, 17472:18496]
    sb["eC"] = SCR[:, 26688:27712]
    sb["eD"] = CP[:, 6224:7248]
    sb["wo_sb"] = [SCR[:, 18496:22592], SCR[:, 22592:26688]]
    sb["d2T"] = [SCR[:, 0:1024], SCR[:, 1024:2048]]
    sb["r1bf"] = [SCR[:, 2048:3072], SCR[:, 3072:4096]]
    sb["xTsl"] = [scr32[:, 2048:3072], scr32[:, 3072:4096]]
    sb["r1ev"] = [scr32[:, 4096:5120], scr32[:, 5120:6144]]
    sb["mrep"] = scr32[:, 6144:7168]
    sb["rrep"] = scr32[:, 7168:8192]
    r132 = R1.bitcast(F32)
    sb["r1ap"] = [r132[:, 8768 + i * 1024: 8768 + (i + 1) * 1024] for i in range(6)] \
        + [scr32[:, 4096:5120], scr32[:, 5120:6144]]
    sb["r1ap_keys"] = ["rA0", "rA1", "rA2", "rA3", "rA4", "rA5", "r1e0", "r1e1"]
    sb["ctxT"] = R1
    sb["bcat_sb"] = R1[0:32, 0:4096]
    sb["ztf"] = r132[0:36, 2048:3072]
    sb["al32"] = r132[0:32, 3072:4096]
    sb["alw"] = r132[0:4, 4096:5120]
    sb["lnA"] = r132[0:1, 5120:6144]
    sb["lnB"] = r132[0:1, 6144:7168]
    sb["asm"] = R1[:, 16384:17536]
    sb["q"] = R2
    sb["x1T"] = R2
    r232 = R2.bitcast(F32)
    sb["wkR2"] = [R2[:, i * 8192:(i + 1) * 8192] for i in range(4)]
    sb["hstg"] = [r232[:, 8192:12288], r232[:, 12288:16384]]
    sb["wqC"] = R2[:, 24576:32768]
    sb["wqD"] = R2[:, 16384:24576]
    sb["ctxnat"] = [[R2[:, h * 1024 + k * 128: h * 1024 + (k + 1) * 128]
                     for k in range(8)] for h in range(H)]
    cp32 = CP.bitcast(F32)
    sb["ropeC"] = cp32[:, 0:1024]
    sb["ropeS"] = cp32[:, 1024:2048]
    sb["bv"] = cp32[:, 2048:3072]
    sb["bqT"] = cp32[:, 3072:3104]
    sb["bkT"] = cp32[:, 3104:3112]
    sb["masks"] = CP[:, 8192:8704]
    sb["identcv"] = CP[:, 8704:8960]
    sb["ident"] = CP[:, 8704:8832]
    sb["onesc"] = CP[:, 8832:8960]
    cfs = sb["cfs"]
    sb["boT"] = cfs[:, 0:32]
    sb["bbT"] = cfs[:, 32:64]
    sb["ones1f"] = cfs[0:1, 64:192]
    sb["ones4f"] = cfs[0:4, 64:65]
    sb["ones14"] = cfs[0:1, 64:68]
    sb["E432"] = cfs[0:4, 192:224]
    sb["rbp4"] = cfs[0:4, 224:225]
    sb["prp4"] = cfs[0:4, 225:226]
    sb["epsb"] = cfs[0:1, 226:227]
    return sb


def alloc_psums(nc, sb):
    sb["scA"] = nc.alloc_psum_tensor("scA", [128, 1024], F32).ap()
    sb["scB"] = nc.alloc_psum_tensor("scB", [128, 1024], F32).ap()
    sb["scC"] = nc.alloc_psum_tensor("scC", [128, 1024], F32).ap()
    sb["pcP"] = nc.alloc_psum_tensor("pcP", [128, 1024], F32).ap()
    sb["ps512"] = [t[:, c:c + 512]
                   for t in (sb["scA"], sb["scB"], sb["scC"], sb["pcP"])
                   for c in (0, 512)]
    sb["ps512k"] = ["psA0", "psA1", "psB0", "psB1", "psC0", "psC1", "psP0", "psP1"]
    sb["pctx"] = [(f"pctx{k}", sb["pcP"][:, k * 256: k * 256 + 129]) for k in range(4)]
    sb["pso"] = [("psoA", sb["scC"][:, 0:512]), ("psoB", sb["scC"][:, 512:1024])]
    sb["meanps"] = sb["scA"][0:1, 0:1024]
    sb["varps"] = sb["scB"][0:1, 0:1024]
    sb["psbc"] = sb["scB"]
    sb["psz"] = sb["scA"][0:36, 0:1024]


def emit_rope_evac(p, sb, ps_key, ps_ap, dst_ap, dst_keys, bias_ap, half, par,
                   tag, war_vstg=False):
    e1 = sb["ropews"]["e1"][par]
    sh = sb["ropews"]["sh"][par]
    c_ap = sb["ropeC"][:, half * 512: half * 512 + 512]
    s_ap = sb["ropeS"][:, half * 512: half * 512 + 512]
    ek, shk = f"re1_{par}", f"rsh_{par}"
    vw = ["vstg"] if war_vstg else []
    p.add("act", lambda e: e.activation(e1, ps_ap, AF.Identity, bias=bias_ap),
          reads=[ps_key, "c_b"], writes=[ek] + vw, tag=f"{tag}e1")
    p.add("sp", lambda e: e.dma_start(out=sh[0:64, :], in_=e1[64:128, :]),
          reads=[ek], writes=[shk] + vw, dma=True, tag=f"{tag}sl")
    p.add("sp", lambda e: e.dma_start(out=sh[64:128, :], in_=e1[0:64, :]),
          reads=[ek], writes=[shk] + vw, dma=True, tag=f"{tag}sh")
    p.add("dve", lambda e: e.tensor_mul(sh, sh, s_ap),
          reads=[shk, "c_rope"], writes=[shk], tag=f"{tag}t2")
    p.add("dve", lambda e: e.tensor_mul(e1, e1, c_ap),
          reads=[ek, "c_rope"], writes=[ek], tag=f"{tag}t1")
    p.add("dve", lambda e: e.tensor_add(dst_ap, e1, sh),
          reads=[ek, shk], writes=list(dst_keys), tag=f"{tag}su")


def p2_group_load(p, sb, io, g, eng):
    par = g % 2
    Kg = [sb["KgA"], sb["KgB"]]
    Vg = [sb["VgA"], sb["VgB"]]
    kx = ["wqA"] if g < 2 else []
    vx = (["wqB"] + (["re1_0"] if g == 1 else [])) if g < 2 else []
    for j in range(4):
        p.add(eng, lambda e, g=g, j=j, par=par: e.dma_start(
            out=Kg[par].rearrange("p (b m) -> p b m", m=128)[:, j:32:4, :],
            in_=io["kv_gK"][j * 1024 + g * 128: j * 1024 + (g + 1) * 128, :]
                .rearrange("p (k m) -> p k m", m=128)),
            reads=[f"kg{par}", "kvgK"],
            writes=[f"kg{par}"] + (kx if j == 0 else []), dma=True, tag=f"ldK{g}_{j}")
        p.add(eng, lambda e, g=g, j=j, par=par: e.dma_start(
            out=Vg[par].rearrange("p (b c) -> p b c", c=129)[:, j:32:4, :],
            in_=io["kv_gV"][j * 1024:(j + 1) * 1024, g * 129:(g + 1) * 129]
                .rearrange("(k p) c -> p k c", p=128)),
            reads=[f"vg{par}", "kvgV"],
            writes=[f"vg{par}"] + (vx if j == 0 else []), dma=True, tag=f"ldV{g}_{j}")


def build_p1(p, io, sb, mid_cb=None):
    wb = [sb["wbufA"], sb["wbufB"]]
    ps = sb["ps512"]
    psk = sb["ps512k"]
    p.add("pool", lambda e: e.dma_start(
        out=wb[0][:].rearrange("p (c m) -> p c m", c=32),
        in_=io["wv"][:, 0:256].rearrange("(c p) m -> p c m", p=128)),
        reads=["wb0"], writes=["wb0"], dma=True, tag="ldwv0")
    for grp in range(4):
        p.add("pool", lambda e, grp=grp: e.dma_start(
            out=sb["hT"][:, grp * 4096:(grp + 1) * 4096].rearrange(
                "p (c m) -> p c m", c=4),
            in_=io["hT"][grp * 512:(grp + 1) * 512, :].rearrange(
                "(c p) m -> p c m", p=128)),
            writes=[f"hT{grp}"], dma=True, tag=f"ldh{grp}")
    p.add("sp", lambda e: e.dma_start(out=sb["ropeC"], in_=io["ropeC"]),
          writes=["c_rope"], dma=True, tag="ld_rc")
    p.add("sp", lambda e: e.dma_start(out=sb["ropeS"], in_=io["ropeS"]),
          writes=["c_rope"], dma=True, tag="ld_rs")
    p.add("sp", lambda e: e.dma_start(out=sb["bv"], in_=io["bv_rep"]),
          writes=["c_bv"], dma=True, tag="ld_bv")
    p.add("sp", lambda e: e.dma_start(out=sb["bqT"], in_=io["bqT"]),
          writes=["c_b"], dma=True, tag="ld_bq")
    p.add("sp", lambda e: e.dma_start(out=sb["bkT"], in_=io["bkT"]),
          writes=["c_b"], dma=True, tag="ld_bk")
    for grp in range(4, 8):
        par = grp % 2
        stg, skey = sb["hstg"][par], f"hstg{par}"
        p.add("sp", lambda e, grp=grp, stg=stg: e.dma_start(
            out=stg[:].rearrange("p (c m) -> p c m", c=4),
            in_=io["hT"][grp * 512:(grp + 1) * 512, :].rearrange(
                "(c p) m -> p c m", p=128)),
            reads=[skey], writes=[skey], dma=True, tag=f"ldhs{grp}")
        p.add("act", lambda e, grp=grp, stg=stg: e.activation(
            sb["hT"][:, grp * 4096:(grp + 1) * 4096], stg[:], AF.Identity),
            reads=[skey], writes=[f"hT{grp}"], tag=f"hcast{grp}")
    vst = sb["v_stage"]
    p.add("dve", lambda e: e.memset(
        vst.rearrange("p (t b c) -> p t b c", b=8, c=129)[:, :, :, 128:129], 1.0),
        writes=["vstg"], tag="ones_v")
    for q in range(4):
        par = q % 2
        if q > 0:
            p.add("pool", lambda e, q=q, par=par: e.dma_start(
                out=wb[par][:].rearrange("p (c m) -> p c m", c=32),
                in_=io["wv"][:, q * 256:(q + 1) * 256].rearrange("(c p) m -> p c m", p=128)),
                reads=[f"wb{par}"], writes=[f"wb{par}"], dma=True, tag=f"ldwv{q}")
        for tt in range(8):
            pp = (q * 8 + tt) % 8
            for c in range(32):
                p.add("pe", lambda e, par=par, tt=tt, c=c, pp=pp: e.matmul(
                    ps[pp][:, 0:256],
                    sb["hT"][:, c * 1024 + tt * 128: c * 1024 + tt * 128 + 128],
                    wb[par][:, c * 256:(c + 1) * 256],
                    start=(c == 0), stop=(c == 31)),
                    reads=[f"hT{c // 4}", f"wb{par}"], writes=[psk[pp]],
                    tag=f"vmm{q}_{tt}_{c}")
            dst = vst[:, tt * 1032:(tt + 1) * 1032].rearrange(
                "p (b c) -> p b c", c=129)[:, 2 * q:2 * q + 2, 0:128]
            p.add("dve", lambda e, q=q, pp=pp, dst=dst: e.tensor_add(
                dst, ps[pp][:, 0:256].rearrange("p (b c) -> p b c", c=128),
                sb["bv"][:, q * 256:(q + 1) * 256].rearrange("p (b c) -> p b c", c=128)),
                reads=[psk[pp], "c_bv"], writes=["vstg"], tag=f"vev{q}_{tt}")
    for gp in range(4):
        extra = ["hstg0"] if gp == 2 else (["hstg1"] if gp == 3 else [])
        p.add("pool", lambda e, gp=gp: e.dma_start(
            out=sb["wkR2"][gp][:].rearrange("p (c m) -> p c m", c=32),
            in_=io["wk"][:, gp * 256:(gp + 1) * 256].rearrange("(c p) m -> p c m", p=128)),
            writes=[f"wkR2_{gp}"] + extra, dma=True, tag=f"ldwk{gp}")
    p.add("sp", lambda e: e.dma_start(
        out=io["kv_ownV"][:].rearrange("(t p) c -> p t c", p=128),
        in_=vst.rearrange("p (t c) -> p t c", t=8)),
        reads=["vstg"], writes=["kvownV"], dma=True, tag="st_v")
    p.add("pool", lambda e: e.collective_compute(
        "AllGather", ALU.bypass, replica_groups=[[0, 1, 2, 3], [4, 5, 6, 7]],
        ins=[io["kv_ownV"].opt()], outs=[io["kv_gV"].opt()]),
        reads=["kvownV"], writes=["kvgV"], tag="agV")
    for gp in range(4):
        for g in (2 * gp, 2 * gp + 1):
            for half in range(2):
                pp = (g * 2 + half) % 8
                for c in range(32):
                    p.add("pe", lambda e, gp=gp, g=g, half=half, c=c, pp=pp: e.matmul(
                        ps[pp][:],
                        sb["wkR2"][gp][:, c * 256 + (g % 2) * 128: c * 256 + (g % 2) * 128 + 128],
                        sb["hT"][:, c * 1024 + half * 512: c * 1024 + half * 512 + 512],
                        start=(c == 0), stop=(c == 31)),
                        reads=[f"hT{c // 4}", f"wkR2_{gp}"], writes=[psk[pp]],
                        tag=f"kmm{g}_{half}_{c}")
                emit_rope_evac(p, sb, psk[pp], ps[pp][:],
                               sb["kstage"][g % 2][:, half * 512: half * 512 + 512],
                               [f"kst{g % 2}"], sb["bkT"][:, g:g + 1], half,
                               (g * 2 + half) % 4, f"k{g}{half}", war_vstg=(g < 2))
            p.add("sp", lambda e, g=g: e.dma_start(
                out=io["kv_ownK"][g * 128:(g + 1) * 128, :], in_=sb["kstage"][g % 2][:]),
                reads=[f"kst{g % 2}"], writes=["kvownK"], dma=True, tag=f"stk{g}")
    QROT = [0, 1, 2, 3, 0, 1, 2, 0, 1, 2, 0, 1, 0, 1, 0, 1]
    wqb = [sb["wbufA"], sb["wbufB"], sb["wqC"], sb["wqD"]]
    wqk = ["wqA", "wqB", "wqC", "wqD"]
    wqx = {0: ["wb0"], 1: ["wb1"], 2: ["wkR2_3"], 3: ["wkR2_2"]}
    for hp in range(16):
        if hp == 4 and mid_cb is not None:
            mid_cb()
        r = QROT[hp]
        p.add("pool", lambda e, hp=hp, r=r: e.dma_start(
            out=wqb[r][:].rearrange("p (c m) -> p c m", c=32),
            in_=io["wq"][:, hp * 256:(hp + 1) * 256].rearrange("(c p) m -> p c m", p=128)),
            writes=[wqk[r]] + wqx[r], dma=True, tag=f"ldwq{hp}")
        for h in (2 * hp, 2 * hp + 1):
            for half in range(2):
                pp = (h * 2 + half) % 8
                for c in range(32):
                    p.add("pe", lambda e, r=r, half=half, c=c, pp=pp, h=h: e.matmul(
                        ps[pp][:],
                        wqb[r][:, c * 256 + (h % 2) * 128: c * 256 + (h % 2) * 128 + 128],
                        sb["hT"][:, c * 1024 + half * 512: c * 1024 + half * 512 + 512],
                        start=(c == 0), stop=(c == 31)),
                        reads=[f"hT{c // 4}", wqk[r]], writes=[psk[pp]],
                        tag=f"qmm{h}_{half}_{c}")
                emit_rope_evac(p, sb, psk[pp], ps[pp][:],
                               sb["q"][:, h * 1024 + half * 512: h * 1024 + half * 512 + 512],
                               [f"q_{h}_{half * 4 + kk}" for kk in range(4)]
                               + [f"wkR2_{h // 8}"],
                               sb["bqT"][:, h:h + 1], half, (h * 2 + half) % 4,
                               f"q{h}{half}")
    p2_group_load(p, sb, io, 0, "pool")


def build_p2(p, io, sb):
    Kg = [sb["KgA"], sb["KgB"]]
    Vg = [sb["VgA"], sb["VgB"]]
    e_sb = [sb["eA"], sb["eB"], sb["eC"], sb["eD"]]
    sc = [sb["scA"], sb["scB"], sb["scC"]]
    sck = ["scA", "scB", "scC"]
    pctx = sb["pctx"]
    q = sb["q"]

    for g in range(8):
        par = g % 2
        if 1 <= g + 1 < 8:
            p2_group_load(p, sb, io, g + 1, "sp")
        for h4 in range(4):
            h = 4 * g + h4
            for s in range(2):
                nb = 16 if s == 0 else 32
                hi = 4 * s + 3
                pend = []
                for pr in range(nb // 2):
                    b0 = 2 * pr
                    cs = pr % 3
                    ce = pr % 4
                    lo = (b0 // 4) if s == 0 else max(4, b0 // 4)
                    ncols = (hi - lo + 1) * 128
                    qkeys = [f"q_{h}_{k}" for k in range(lo, hi + 1)]
                    for half in range(2):
                        b = b0 + half
                        p.add("pe", lambda e, b=b, lo=lo, ncols=ncols, half=half,
                              cs=cs, h=h, par=par: e.matmul(
                            sc[cs][:, half * 512: half * 512 + ncols],
                            Kg[par][:, b * 128:(b + 1) * 128],
                            q[:, h * 1024 + lo * 128: h * 1024 + lo * 128 + ncols],
                            start=True, stop=True),
                            reads=[f"kg{par}"] + qkeys, writes=[sck[cs]],
                            tag=f"sc{h}_{s}_{b}")
                    src = sc[cs].rearrange("p (x c) -> p x c", x=2)[:, :, 0:ncols]
                    dst = e_sb[ce].rearrange("p (x c) -> p x c", x=2)[:, :, 0:ncols]
                    p.add("act", lambda e, src=src, dst=dst: e.activation(
                        dst, src, AF.Exp, scale=SCALE),
                        reads=[sck[cs]], writes=[f"e{ce}"], tag=f"ex{h}_{s}_{b0}")
                    if (s == 0) or (b0 >= 16):
                        for half in range(2):
                            r = (b0 + half) % 4
                            p.add("dve", lambda e, ce=ce, half=half, r=r: e.tensor_mul(
                                e_sb[ce][:, half * 512: half * 512 + 128],
                                e_sb[ce][:, half * 512: half * 512 + 128],
                                sb["masks"][:, r * 128:(r + 1) * 128]),
                                reads=[f"e{ce}", "c_mi"], writes=[f"e{ce}"],
                                tag=f"mk{h}_{s}_{b0 + half}")
                    if len(pend) == 2:
                        pend.pop(0)()
                    def make_ctx(ce=ce, lo=lo, b0=b0, s=s, par=par, h=h, hi=hi):
                        def emit_ctx():
                            for half in range(2):
                                b = b0 + half
                                for k in range(lo, hi + 1):
                                    kp = k - 4 * s
                                    kk, kap = pctx[kp]
                                    st = (b == 0 and kp % 2 == 0)
                                    ws = [kk] + ([pctx[kp + 1][0]] if st else [])
                                    p.add("pe", lambda e, ce=ce, half=half, k=k,
                                          lo=lo, b=b, kap=kap, par=par, st=st: e.matmul(
                                        kap,
                                        e_sb[ce][:, half * 512 + (k - lo) * 128:
                                                  half * 512 + (k - lo + 1) * 128],
                                        Vg[par][:, b * 129:(b + 1) * 129],
                                        start=st, stop=(b == 4 * k + 3),
                                        skip_group_check=True),
                                        reads=[f"e{ce}", f"vg{par}"], writes=ws,
                                        tag=f"cx{h}_{s}_{b}_{k}")
                        return emit_ctx
                    pend.append(make_ctx())
                for f in pend:
                    f()
                for k in range(4 * s, 4 * s + 4):
                    kk, kap = pctx[k - 4 * s]
                    p.add("dve", lambda e, kap=kap, k=k: e.reciprocal(
                        sb["rcp"][:, k:k + 1], kap[:, 128:129]),
                        reads=[kk], writes=["rcp"], tag=f"rc{h}_{k}")
                    p.add("act", lambda e, kap=kap, k=k, h=h: e.activation(
                        sb["ctxnat"][h][k], kap[:, 0:128], AF.Identity,
                        scale=sb["rcp"][:, k:k + 1]),
                        reads=[kk, "rcp"], writes=[f"q_{h}_{k}"], tag=f"cn{h}_{k}")
                    p.add("sp", lambda e, h=h, k=k: e.dma_start(
                        out=sb["ctxT"][:, h * 1024 + k * 128: h * 1024 + (k + 1) * 128],
                        in_=sb["ctxnat"][h][k], transpose=True),
                        reads=[f"q_{h}_{k}"], writes=["ctxT"], dma=True,
                        tag=f"tx{h}_{k}")


def build_p3t(p, io, sb):
    wo_sb = sb["wo_sb"]
    xT = sb["xTsl"]
    r1ev = sb["r1ev"]
    r1bf = sb["r1bf"]
    d2T = sb["d2T"]
    pso = sb["pso"]

    def mm_stage(which, rhs, rhs_key_fn, w_src, bias_ap, x_src, x_key, r_dst,
                 r_key, lora, bf16_to_R2=False, norm_x=False):
        for oc in range(32):
            par = oc % 2
            p.add("pool", lambda e, oc=oc, par=par: e.dma_start(
                out=wo_sb[par][:].rearrange("p (c m) -> p c m", c=32),
                in_=w_src[:, oc * 128:(oc + 1) * 128].rearrange("(c p) m -> p c m", p=128)),
                reads=[f"wo{par}"], writes=[f"wo{par}"], dma=True, tag=f"{which}w{oc}")
            extra = ["vg0", "vg1", "e0", "e1"] if (which == "A" and oc < 2) else []
            p.add("sp", lambda e, oc=oc, par=par: e.dma_start(
                out=xT[par][:], in_=x_src[oc * 128:(oc + 1) * 128, :]),
                reads=[f"xT{par}"] + ([f"{x_key}_{oc}"] if x_key else []) + extra,
                writes=[f"xT{par}"], dma=True, tag=f"{which}x{oc}")
            if norm_x:
                p.add("dve", lambda e, par=par: e.tensor_sub(
                    xT[par][:], xT[par][:], sb["mrep"]),
                    reads=[f"xT{par}", "mrep"], writes=[f"xT{par}"], tag=f"{which}nx{oc}")
                p.add("dve", lambda e, par=par: e.tensor_mul(
                    xT[par][:], xT[par][:], sb["rrep"]),
                    reads=[f"xT{par}", "rrep"], writes=[f"xT{par}"], tag=f"{which}ny{oc}")
            for half in range(2):
                pk, pap = pso[half]
                for c in range(32):
                    pw = [pk] + (["scC"] if (which == "A" and oc == 0 and c == 0)
                                 else [])
                    p.add("pe", lambda e, par=par, half=half, c=c, pap=pap: e.matmul(
                        pap,
                        wo_sb[par][:, c * 128:(c + 1) * 128],
                        rhs[:, c * 1024 + half * 512: c * 1024 + half * 512 + 512],
                        start=(c == 0), stop=(c == 31 and not lora)),
                        reads=[rhs_key_fn(c), f"wo{par}"], writes=pw,
                        tag=f"{which}m{oc}_{half}_{c}")
                if lora:
                    p.add("pe", lambda e, oc=oc, half=half, pap=pap: e.matmul(
                        pap, sb["bcat_sb"][0:32, oc * 128:(oc + 1) * 128],
                        sb["zwT"][0:32, half * 512: half * 512 + 512],
                        start=False, stop=True),
                        reads=["c_bc", "zwT"], writes=[pk], tag=f"lr{oc}_{half}")
                p.add("dve", lambda e, oc=oc, par=par, half=half, pap=pap:
                    e.scalar_tensor_tensor(
                        r1ev[par][:, half * 512: half * 512 + 512], pap,
                        bias_ap[:, oc:oc + 1],
                        xT[par][:, half * 512: half * 512 + 512],
                        ALU.add, ALU.add),
                    reads=[pso[half][0], f"xT{par}", "c_f"], writes=[f"r1e{par}"],
                    tag=f"{which}e{oc}_{half}")
            p.add("sp", lambda e, oc=oc, par=par: e.dma_start(
                out=r_dst[oc * 128:(oc + 1) * 128, :], in_=r1ev[par][:]),
                reads=[f"r1e{par}"], writes=[f"{r_key}_{oc}"], dma=True,
                tag=f"{which}s{oc}")
            if bf16_to_R2:
                bdst, bkey = sb["x1T"][:, oc * 1024:(oc + 1) * 1024], f"x1T_{oc}"
            else:
                bdst, bkey = r1bf[par][:], f"r1b{par}"
            p.add("act", lambda e, bdst=bdst, par=par: e.activation(
                bdst, r1ev[par][:], AF.Identity),
                reads=[f"r1e{par}"], writes=[bkey], tag=f"{which}b{oc}")
            p.add("act", lambda e, par=par: e.activation(
                d2T[par][:], r1ev[par][:], AF.Square),
                reads=[f"r1e{par}"], writes=[f"d2{par}"], tag=f"{which}d{oc}")
            for half in range(2):
                bsrc = bdst if bf16_to_R2 else r1bf[par][:]
                p.add("pe", lambda e, oc=oc, half=half, bsrc=bsrc: e.matmul(
                    sb["meanps"][:, half * 512: half * 512 + 512],
                    sb["onesc"][:, 0:1],
                    bsrc[:, half * 512: half * 512 + 512],
                    start=(oc == 0), stop=(oc == 31)),
                    reads=[bkey, "c_mi"], writes=["scA"],
                    tag=f"{which}sm{oc}_{half}")
                p.add("pe", lambda e, oc=oc, par=par, half=half: e.matmul(
                    sb["varps"][:, half * 512: half * 512 + 512],
                    sb["onesc"][:, 0:1],
                    d2T[par][:, half * 512: half * 512 + 512],
                    start=(oc == 0), stop=(oc == 31)),
                    reads=[f"d2{par}", "c_mi"], writes=["scB"],
                    tag=f"{which}sv{oc}_{half}")

    def ln_stats_apply(which, r_src, r_key, dst_bf16, dst_bf16_key, dst_f32_dram,
                       dst_f32_key, out_flag):
        lnA, lnB, lnC = sb["lnA"], sb["lnB"], sb["alw"][0:1, :]
        p.add("dve", lambda e: e.tensor_scalar(
            lnA, sb["meanps"], 1.0 / 4096.0, None, ALU.mult),
            reads=["scA"], writes=["lnm"], tag=f"{which}m")
        p.add("dve", lambda e: e.tensor_mul(lnB, lnA, lnA),
            reads=["lnm"], writes=["lnm2"], tag=f"{which}m2")
        p.add("dve", lambda e: e.scalar_tensor_tensor(
            lnB, sb["varps"], 1.0 / 4096.0, lnB, ALU.mult, ALU.subtract),
            reads=["scB", "lnm2"], writes=["lnv"], tag=f"{which}v")
        p.add("act", lambda e: e.activation(
            lnC, lnB, AF.Ln, bias=sb["epsb"]),
            reads=["lnv", "c_f", "alg"], writes=["lnl"], tag=f"{which}l")
        p.add("act", lambda e: e.activation(
            lnB, lnC, AF.Exp, scale=-0.5),
            reads=["lnl"], writes=["lnr"], tag=f"{which}r")
        for half in range(2):
            p.add("pe", lambda e, half=half: e.matmul(
                sb["psbc"][:, half * 512: half * 512 + 512],
                sb["ones1f"], lnA[:, half * 512: half * 512 + 512],
                start=True, stop=True),
                reads=["lnm", "c_f"], writes=["scB"], tag=f"{which}bm{half}")
        p.add("dve", lambda e: e.tensor_copy(sb["mrep"], sb["psbc"][:, 0:1024]),
              reads=["scB"], writes=["mrep"], tag=f"{which}cm")
        for half in range(2):
            p.add("pe", lambda e, half=half: e.matmul(
                sb["psbc"][:, half * 512: half * 512 + 512],
                sb["ones1f"], lnB[:, half * 512: half * 512 + 512],
                start=True, stop=True),
                reads=["lnr", "c_f", "mrep"], writes=["scB"], tag=f"{which}br{half}")
        p.add("dve", lambda e: e.tensor_copy(sb["rrep"], sb["psbc"][:, 0:1024]),
              reads=["scB"], writes=["rrep"], tag=f"{which}cr")
        if dst_f32_dram is None:
            for oc in range(32):
                ch = sb["x1T"][:, oc * 1024:(oc + 1) * 1024]
                p.add("dve", lambda e, ch=ch: e.tensor_sub(ch, ch, sb["mrep"]),
                    reads=[f"x1T_{oc}", "mrep"], writes=[f"x1T_{oc}"],
                    tag=f"{which}as{oc}")
                p.add("dve", lambda e, ch=ch: e.tensor_mul(ch, ch, sb["rrep"]),
                    reads=[f"x1T_{oc}", "rrep"], writes=[f"x1T_{oc}"],
                    tag=f"{which}am{oc}")
            return
        r1a = sb["r1ap"]
        rk = sb["r1ap_keys"]

        def emit_load(oc):
            par = oc % 8
            p.add("sp", lambda e, oc=oc, buf=r1a[par]: e.dma_start(
                out=buf[:], in_=r_src[oc * 128:(oc + 1) * 128, :]),
                reads=[rk[par], f"{r_key}_{oc}"], writes=[rk[par]], dma=True,
                tag=f"{which}al{oc}")

        for oc in range(8):
            emit_load(oc)
        for oc in range(32):
            par = oc % 8
            buf = r1a[par]
            p.add("dve", lambda e, buf=buf: e.tensor_sub(buf[:], buf[:], sb["mrep"]),
                reads=[rk[par], "mrep"], writes=[rk[par]], tag=f"{which}as{oc}")
            p.add("dve", lambda e, buf=buf: e.tensor_mul(buf[:], buf[:], sb["rrep"]),
                reads=[rk[par], "rrep"], writes=[rk[par]], tag=f"{which}am{oc}")
            p.add("pool", lambda e, oc=oc, buf=buf: e.dma_start(
                out=dst_f32_dram[oc * 128:(oc + 1) * 128, :], in_=buf[:]),
                reads=[rk[par]], writes=[f"{dst_f32_key}_{oc}"], dma=True,
                out=out_flag, tag=f"{which}af{oc}")
            if oc + 8 < 32:
                emit_load(oc + 8)

    mm_stage("A", sb["ctxT"], lambda c: "ctxT", io["wo"], sb["boT"], io["hT"],
             None, io["r1_dram"], "r1d", False, bf16_to_R2=True)
    p.fence()
    p.add("pool", lambda e: e.dma_start(out=sb["bcat_sb"], in_=io["bcat"]),
          writes=["c_bc"], dma=True, tag="ld_bc")
    p.add("pool", lambda e: e.dma_start(
        out=sb["asm"][:].rearrange("p (c m) -> p c m", c=32),
        in_=io["asmall"][:].rearrange("(c p) m -> p c m", p=128)),
        writes=["c_as"], dma=True, tag="ld_as")
    ln_stats_apply("L1", io["r1_dram"], "r1d", sb["x1T"],
                   lambda oc: f"x1T_{oc}", None, None, False)
    for half in range(2):
        for c in range(32):
            p.add("pe", lambda e, c=c, half=half: e.matmul(
                sb["psz"][:, half * 512: half * 512 + 512],
                sb["asm"][:, c * 36:(c + 1) * 36],
                sb["x1T"][:, c * 1024 + half * 512: c * 1024 + half * 512 + 512],
                start=(c == 0), stop=(c == 31)),
                reads=["c_as", f"x1T_{c}"], writes=["scA"], tag=f"z{half}_{c}")
    p.add("dve", lambda e: e.tensor_copy(sb["ztf"][0:36, 0:1024],
                                         sb["psz"][0:36, 0:1024]),
          reads=["scA"], writes=["ztf"], tag="zev")
    p.add("sp", lambda e: e.dma_start(out=sb["alw"][0:4, 0:1024],
                                      in_=sb["ztf"][32:36, 0:1024]),
          reads=["ztf"], writes=["alg"], dma=True, tag="alshift")
    al = sb["alw"]
    p.add("dve", lambda e: e.tensor_scalar(
        al[0:4, :], al[0:4, :], sb["rbp4"], None, ALU.add),
        reads=["alg", "c_f"], writes=["alg"], tag="al_rb")
    p.add("act", lambda e: e.activation(al[0:4, :], al[0:4, :], AF.Exp),
          reads=["alg"], writes=["alg"], tag="al_exp")
    p.add("dve", lambda e: e.tensor_scalar(
        al[0:4, :], al[0:4, :], sb["prp4"], None, ALU.mult),
        reads=["alg", "c_rp"], writes=["alg"], tag="al_pr")
    for half in range(2):
        p.add("pe", lambda e, half=half: e.matmul(
            sb["scA"][0:1, half * 512: half * 512 + 512],
            sb["ones4f"], al[0:4, half * 512: half * 512 + 512],
            start=True, stop=True),
            reads=["alg", "c_f"], writes=["scA"], tag=f"alden{half}")
    p.add("act", lambda e: e.activation(sb["lnB"], sb["scA"][0:1, 0:1024], AF.Ln),
          reads=["scA", "lnr"], writes=["alrcp"], tag="alrc1")
    p.add("act", lambda e: e.activation(sb["lnB"], sb["lnB"], AF.Exp, scale=-1.0),
          reads=["alrcp"], writes=["alrcp"], tag="alrc2")
    for half in range(2):
        p.add("pe", lambda e, half=half: e.matmul(
            sb["scB"][0:4, half * 512: half * 512 + 512],
            sb["ones14"], sb["lnB"][:, half * 512: half * 512 + 512],
            start=True, stop=True),
            reads=["alrcp", "c_f"], writes=["scB"], tag=f"albc{half}")
    p.add("dve", lambda e: e.tensor_mul(
        al[0:4, :], al[0:4, :], sb["scB"][0:4, 0:1024]),
        reads=["alg", "scB"], writes=["alg"], tag="al_n")
    for half in range(2):
        p.add("pe", lambda e, half=half: e.matmul(
            sb["scA"][0:32, half * 512: half * 512 + 512],
            sb["E432"], al[0:4, half * 512: half * 512 + 512],
            start=True, stop=True),
            reads=["alg", "c_f"], writes=["scA"], tag=f"alx{half}")
    p.add("dve", lambda e: e.tensor_copy(sb["al32"][0:32, :],
                                         sb["scA"][0:32, 0:1024]),
          reads=["scA"], writes=["al32"], tag="alxc")
    p.add("dve", lambda e: e.scalar_tensor_tensor(
        sb["zwT"][0:32, :], sb["ztf"][0:32, :], 2.0, sb["al32"][0:32, :],
        ALU.mult, ALU.mult),
        reads=["ztf", "al32"], writes=["zwT"], tag="zw")
    mm_stage("B", sb["x1T"], lambda c: f"x1T_{c}", io["wbT"], sb["bbT"],
             io["r1_dram"], "r1d", io["r2_dram"], "r2d", True, norm_x=True)
    ln_stats_apply("L2", io["r2_dram"], "r2d", None, None, io["outT"], "outd", True)


def build_fused():
    nc = bass.Bass("TRN2", target_bir_lowering=False)
    io = {}
    ein = lambda n, s_, d: nc.dram_tensor(n, s_, d, kind="ExternalInput").ap()
    io["hT"] = ein("hT", [4096, 1024], F32)
    io["wq"] = ein("wq", [4096, 4096], F32)
    io["wk"] = ein("wk", [4096, 1024], F32)
    io["wv"] = ein("wv", [4096, 1024], F32)
    io["ropeC"] = ein("ropeC", [128, 1024], F32)
    io["ropeS"] = ein("ropeS", [128, 1024], F32)
    io["bqT"] = ein("bqT", [128, 32], F32)
    io["bkT"] = ein("bkT", [128, 8], F32)
    io["bv_rep"] = ein("bv_rep", [128, 1024], F32)
    io["wo"] = ein("wo", [4096, 4096], F32)
    io["wbT"] = ein("wbT", [4096, 4096], F32)
    io["masks"] = ein("masks", [128, 512], BF16)
    io["identc"] = ein("identc", [128, 256], BF16)
    io["cf32"] = ein("cf32", [128, 256], F32)
    io["asmall"] = ein("asmall", [4096, 36], F32)
    io["bcat"] = ein("bcat", [32, 4096], F32)
    io["outT"] = nc.dram_tensor("outT", [4096, 1024], F32, kind="ExternalOutput").ap()
    io["kv_ownK"] = nc.dram_tensor("kv_ownK", [1024, 1024], BF16, kind="Internal").ap()
    io["kv_ownV"] = nc.dram_tensor("kv_ownV", [1024, 1032], BF16, kind="Internal").ap()
    io["kv_gK"] = nc.dram_tensor("kv_gK", [4096, 1024], BF16, kind="Internal").ap()
    io["kv_gV"] = nc.dram_tensor("kv_gV", [4096, 1032], BF16, kind="Internal").ap()
    io["r1_dram"] = nc.dram_tensor("r1_dram", [4096, 1024], F32, kind="Internal").ap()
    io["r2_dram"] = nc.dram_tensor("r2_dram", [4096, 1024], F32, kind="Internal").ap()
    sb = alloc_bufs(nc)
    alloc_psums(nc, sb)
    sb["hT"] = sb["ctxT"]
    p = Prog(nc)
    p.add("sp", lambda e: e.dma_start(out=sb["masks"], in_=io["masks"]),
          writes=["c_mi"], dma=True, tag="ld_mk")
    p.add("sp", lambda e: e.dma_start(out=sb["identcv"], in_=io["identc"]),
          writes=["c_mi"], dma=True, tag="ld_id")
    p.add("sp", lambda e: e.dma_start(out=sb["cfs"][:], in_=io["cf32"]),
          writes=["c_f"], dma=True, tag="ld_cf")
    p.add("dve", lambda e: e.tensor_scalar(
        sb["prp4"], sb["prp4"], 1e-12, None, ALU.add),
        reads=["c_f"], writes=["c_rp"], tag="preps")

    def collective():
        p.add("pool", lambda e: e.collective_compute(
            "AllGather", ALU.bypass, replica_groups=[[0, 1, 2, 3], [4, 5, 6, 7]],
            ins=[io["kv_ownK"].opt()], outs=[io["kv_gK"].opt()]),
            reads=["kvownK", "kvgV"], writes=["kvgK"], tag="agK")

    build_p1(p, io, sb, mid_cb=collective)
    p.fence()
    build_p2(p, io, sb)
    build_p3t(p, io, sb)
    p.emit()
    return nc


def stage_core_inputs(inputs, c):
    j, bb = c % 4, c // 4
    x = np.asarray(inputs["L0_attn_hidden_states"][bb])
    rows = np.concatenate([np.arange((4 * k + j) * 128, (4 * k + j + 1) * 128)
                           for k in range(8)])
    hT = np.ascontiguousarray(x[rows].T)
    half = D // 2
    inv = 1.0 / (10000.0 ** (np.arange(half, dtype=np.float64) / half))
    ang = rows[:, None].astype(np.float64) * inv[None, :]
    cosT = np.cos(ang).T.astype(np.float32)
    sinT = np.sin(ang).T.astype(np.float32)
    ropeC = np.ascontiguousarray(np.concatenate([cosT, cosT], 0))
    ropeS = np.ascontiguousarray(np.concatenate([-sinT, sinT], 0))
    tri = (np.arange(128)[:, None] <= np.arange(128)[None, :]).astype(np.float32)
    m = np.zeros((128, 512), np.float32)
    for r in range(4):
        if r < j:
            m[:, r * 128:(r + 1) * 128] = 1.0
        elif r == j:
            m[:, r * 128:(r + 1) * 128] = tri
    return dict(
        hT=hT, rows=rows, bb=bb, ropeC=ropeC, ropeS=ropeS,
        masks=np.ascontiguousarray(m.astype(ml_dtypes.bfloat16)),
        bqT=np.ascontiguousarray(np.asarray(inputs["L0_attn_bq"]).reshape(32, 128).T),
        bkT=np.ascontiguousarray(np.asarray(inputs["L0_attn_bk"]).reshape(8, 128).T),
    )


def kernel(**inputs):
    inputs = {k: np.asarray(v) for k, v in inputs.items()}
    cores = list(range(8))
    act = inputs["L0_ffn_active"].astype(np.int64)
    rep = lambda v: np.ascontiguousarray(np.broadcast_to(
        np.asarray(v, np.float32)[None, :], (128, np.asarray(v).shape[0])))
    identc = np.zeros((128, 256), ml_dtypes.bfloat16)
    identc[:, 0:128] = np.eye(128, dtype=ml_dtypes.bfloat16)
    identc[:, 128:256] = np.ones((128, 128), ml_dtypes.bfloat16)
    cf32 = np.zeros((128, 256), np.float32)
    cf32[:, 0:32] = inputs["L0_attn_bo"].reshape(32, 128).T
    cf32[:, 32:64] = inputs["L0_ffn_b_base"].reshape(32, 128).T
    cf32[:, 64:192] = 1.0
    for ee in range(4):
        cf32[ee, 192 + ee * 8:192 + (ee + 1) * 8] = 1.0
    cf32[0:4, 224] = inputs["L0_ffn_router_b"][act].astype(np.float32)
    cf32[0:4, 225] = inputs["L0_ffn_domain_prior"][act].astype(np.float32)
    cf32[:, 226] = EPS_LN
    shared = dict(
        wq=inputs["L0_attn_Wq"], wk=inputs["L0_attn_Wk"], wv=inputs["L0_attn_Wv"],
        wo=inputs["L0_attn_Wo"],
        wbT=np.ascontiguousarray(inputs["L0_ffn_W_base"].T),
        bv_rep=rep(inputs["L0_attn_bv"]),
        cf32=cf32, identc=identc,
        asmall=np.ascontiguousarray(np.concatenate(
            [inputs["L0_ffn_lora_A"][act].reshape(32, HID).T,
             inputs["L0_ffn_router_W"][act].T], axis=1)),
        bcat=np.ascontiguousarray(
            inputs["L0_ffn_lora_B"][act].transpose(0, 2, 1).reshape(32, HID)),
    )
    staged = [stage_core_inputs(inputs, c) for c in cores]
    nc = build_fused()
    in_maps = [{"hT": st["hT"], "wq": shared["wq"], "wk": shared["wk"],
                "wv": shared["wv"], "ropeC": st["ropeC"], "ropeS": st["ropeS"],
                "bqT": st["bqT"], "bkT": st["bkT"], "bv_rep": shared["bv_rep"],
                "wo": shared["wo"], "wbT": shared["wbT"], "masks": st["masks"],
                "identc": shared["identc"], "cf32": shared["cf32"],
                "asmall": shared["asmall"], "bcat": shared["bcat"]}
               for st in staged]
    res = run_bass_kernel_spmd(nc, in_maps, core_ids=cores)
    out = np.zeros((2, S, HID), np.float32)
    for c in cores:
        out[staged[c]["bb"]][staged[c]["rows"]] = np.asarray(res.results[c]["outT"]).T
    return out
